# version 02
# baseline (speedup 1.0000x reference)
import contextlib
import os
import numpy as np
import ml_dtypes
import concourse.bass as bass
import concourse.mybir as mybir
from concourse.bass_utils import run_bass_kernel_spmd

F32 = mybir.dt.float32
BF16 = mybir.dt.bfloat16
U8 = mybir.dt.uint8
AF = mybir.ActivationFunctionType
OP = mybir.AluOpType
AX = mybir.AxisListType

NC = 8
B, S, H = 4, 2048, 4096
NH, D = 32, 128
T = B * S
TO = T // NC
NTC = TO // 128
GTC = T // 128
HPC = NH // NC
IF = 11008
NCH = IF // 128
NCHP = 11
MAGIC = float(1.5 * 2.0 ** 23)
LN127 = float(np.log(np.float32(127.0)))
INVSQ = float(1.0 / np.sqrt(np.float64(D)))
EPS = 1e-5
NEG = -1.0e30

GSZ = NCHP * 128 * H
ZSZ = 32 * NCHP * 128
DSZ = 8 * 128 * NCHP * 512
OFF_G = 0
OFF_U = GSZ
OFF_ZG = 2 * GSZ
OFF_ZU = 2 * GSZ + ZSZ
OFF_D = 2 * GSZ + 2 * ZSZ
FSZ = 2 * GSZ + 2 * ZSZ + DSZ

AWQPOS = [(0, 0), (2, 0), (0, 1), (2, 1), (1, 0), (3, 0), (1, 1), (3, 1)]


def ffn_chunks(c):
    return list(range(NCHP * c, min(NCHP * c + NCHP, NCH)))


def real_counts():
    return [len(ffn_chunks(c)) for c in range(NC)]


class Prog:
    def __init__(self):
        self.items = []
        self.cnt = {}

    def on(self, key, fn):
        self.items.append((key, fn))

    def inc(self, name, n):
        self.cnt[name] = self.cnt.get(name, 0) + n
        return self.cnt[name]

    def val(self, name):
        return self.cnt.get(name, 0)


class Buf:

    def __init__(self, t, w):
        self.t, self.w = t, w

    def __getitem__(self, i):
        return self.t[:, i * self.w:(i + 1) * self.w]


def build_program(build_upto=99, taps=(), no_cc=False, no_pid=False, flat=False):
    nc = bass.Bass("TRN2", target_bir_lowering=False)
    dt = nc.dram_tensor
    P = Prog()

    x_own = dt("x_own", [TO, H], F32, kind="ExternalInput").ap()
    w_in = {k: dt(f"w_{k}", [512, H], F32, kind="ExternalInput").ap() for k in "qkvo"}
    g_qw = dt("g_qw", [H, NCHP * 64], U8, kind="ExternalInput").ap()
    g_sc = dt("g_sc", [32, NCHP * 128], F32, kind="ExternalInput").ap()
    g_qz = dt("g_qz", [32, NCHP * 64], U8, kind="ExternalInput").ap()
    u_qw = dt("u_qw", [H, NCHP * 64], U8, kind="ExternalInput").ap()
    u_sc = dt("u_sc", [32, NCHP * 128], F32, kind="ExternalInput").ap()
    u_qz = dt("u_qz", [32, NCHP * 64], U8, kind="ExternalInput").ap()
    d_qw = dt("d_qw", [NCHP * 128, 2048], U8, kind="ExternalInput").ap()
    d_sc = dt("d_sc", [NCHP, H], F32, kind="ExternalInput").ap()
    d_qz = dt("d_qz", [NCHP, 2048], U8, kind="ExternalInput").ap()
    cos_in = dt("cos_t", [S, D], F32, kind="ExternalInput").ap()
    sin_in = dt("sin_t", [S, D], F32, kind="ExternalInput").ap()
    ident_in = dt("ident", [128, 128], BF16, kind="ExternalInput").ap()
    identf_in = dt("identf", [128, 128], F32, kind="ExternalInput").ap()
    ones_in = dt("ones_f", [1, 128], F32, kind="ExternalInput").ap()
    onesb_in = dt("ones_b", [1, 128], BF16, kind="ExternalInput").ap()
    onesa_in = dt("ones_a", [128, 128], BF16, kind="ExternalInput").ap()
    e64_in = dt("e64", [64, 64 * 128], BF16, kind="ExternalInput").ap()
    mtok_in = dt("mask_tok", [128, 128], F32, kind="ExternalInput").ap()
    mtt_in = dt("mask_T", [128, 128], F32, kind="ExternalInput").ap()
    zc32_in = dt("zc32", [128, 96], BF16, kind="ExternalInput").ap()
    out_own = dt("out_own", [TO, H], F32, kind="ExternalOutput").ap()

    agx_in = dt("agx_in", [NTC, 128, H], BF16, kind="Internal").ap()
    agx_out = dt("agx_out", [NC * NTC, 128, H], BF16, kind="Internal").ap()
    agsx_in = dt("agsx_in", [128, NTC], F32, kind="Internal").ap()
    agsx_out = dt("agsx_out", [NC * 128, NTC], F32, kind="Internal").ap()
    agwo_in = dt("agwo_in", [32, 128, 512], BF16, kind="Internal").ap()
    agwo_out = dt("agwo_out", [NC * 32, 128, 512], BF16, kind="Internal").ap()
    agswo_in = dt("agswo_in", [1, 512], F32, kind="Internal").ap()
    agswo_out = dt("agswo_out", [NC, 512], F32, kind="Internal").ap()
    agg_in = dt("agg_in", [1, GSZ], BF16, kind="Internal").ap()
    agg_out = dt("agg_out", [NC, GSZ], BF16, kind="Internal").ap()
    agu_in = dt("agu_in", [1, GSZ], BF16, kind="Internal").ap()
    agu_out = dt("agu_out", [NC, GSZ], BF16, kind="Internal").ap()
    agz_in = dt("agz_in", [1, 2 * ZSZ], BF16, kind="Internal").ap()
    agz_out = dt("agz_out", [NC, 2 * ZSZ], BF16, kind="Internal").ap()
    agd_in = dt("agd_in", [1, DSZ], BF16, kind="Internal").ap()
    agd_out = dt("agd_out", [NC, DSZ], BF16, kind="Internal").ap()
    aga_in = dt("aga_in", [NC * 512, TO], BF16, kind="Internal").ap()
    aga_out = dt("aga_out", [NC * NC * 512, TO], BF16, kind="Internal").ap()
    agal = dt("agal", [NC, 512, TO], BF16, kind="Internal").ap()
    ags_in = dt("ags_in", [NC * 128, 32], F32, kind="Internal").ap()
    ags_out = dt("ags_out", [NC * NC * 128, 32], F32, kind="Internal").ap()
    agsl = dt("agsl", [NC, 128, 32], F32, kind="Internal").ap()
    qt_d = dt("qt_d", [HPC, 128, T], BF16, kind="Internal").ap()
    kt_d = dt("kt_d", [HPC, 128, T], BF16, kind="Internal").ap()
    vt_d = dt("vt_d", [T, HPC * 128], BF16, kind="Internal").ap()
    yf_d = dt("yf_d", [NTC, 128, H], F32, kind="Internal").ap()
    yt_d = dt("yt_d", [NTC, 128, H], BF16, kind="Internal").ap()
    ait_d = dt("ait_d", [NTC, 128, H], BF16, kind="Internal").ap()
    ht_d = dt("ht_d", [NCH, 128, TO], BF16, kind="Internal").ap()

    tap_t = {}
    for name, shape, dty in [
        ("xint", [NTC, 128, H], BF16), ("qt", [HPC, 128, T], BF16),
        ("kt", [HPC, 128, T], BF16), ("vt", [T, HPC * 128], BF16),
        ("attn", [HPC * 128, T], BF16), ("ssum", [128, 256], F32),
        ("y", [NTC, 128, H], F32), ("ht", [NCH, 128, TO], BF16),
        ("agf", [1, FSZ], BF16), ("aint", [NTC, 128, H], BF16),
        ("fqraw", [NTC, 128, H], BF16), ("cols", [NTC, 128, 16], F32),
        ("o1", [NTC, 128, H], F32),
    ]:
        if name in taps:
            tap_t[name] = dt("tap_" + name, shape, dty, kind="ExternalOutput").ap()

    agxv = agx_out.rearrange("(r c) p f -> r c p f", r=NC)
    agsxv = agsx_out.rearrange("(r p) f -> r p f", r=NC)
    agwov = agwo_out.rearrange("(r g) p f -> r g p f", r=NC)
    agai = aga_in.rearrange("(d g) t -> d g t", d=NC)
    agav = aga_out.rearrange("(x g) t -> x g t", x=NC * NC)
    agsi = ags_in.rearrange("(d p) f -> d p f", d=NC)
    agsv = ags_out.rearrange("(x p) f -> x p f", x=NC * NC)

    est = contextlib.ExitStack()
    sbt = nc.sbuf_tensor

    def alloc(*args):
        return est.enter_context(sbt(*args))

    with est, \
         nc.psum_tensor("PS", [128, 3072], F32) as PS, \
         nc.psum_tensor("PSB", [128, 2048], BF16) as PSB, \
         nc.semaphore("s_ld") as s_ld, \
         nc.semaphore("s_ld2") as s_ld2, \
         nc.semaphore("s_ld3") as s_ld3, \
         nc.semaphore("s_mm") as s_mm, \
         nc.semaphore("s_ve") as s_ve, \
         nc.semaphore("s_sc") as s_sc, \
         nc.semaphore("s_cc") as s_cc, \
         nc.semaphore("s_st") as s_st, \
         nc.semaphore("s_pz") as s_pz, \
         nc.Block() as block:
        SEM = dict(ld=s_ld, ld2=s_ld2, ld3=s_ld3, mm=s_mm, ve=s_ve, sc=s_sc,
                   cc=s_cc, st=s_st, pz=s_pz)

        def bank(i):
            return PS[:, 512 * i:512 * (i + 1)]

        pid_cache = {}

        def get_pid(e):
            if "v" not in pid_cache:
                pid_cache["v"] = e.partition_id()
            return pid_cache["v"]

        if not no_pid:
            P.on("sync", lambda e, s: (get_pid(e), None)[1])

        DMA_SERIAL = bool(int(os.environ.get("KSERIAL", "0")))

        def dma(q, out, in_, sem=None):
            if sem is None:
                sem = {"sync": "ld", "scalar": "ld2", "gpsimd": "ld3"}[q]
            v = P.inc(sem, 16)
            P.on(q, lambda e, s, out=out, in_=in_, sem=sem:
                 e.dma_start(out=out, in_=in_).then_inc(s[sem], 16))
            if DMA_SERIAL:
                P.on(q, lambda e, s, sem=sem, v=v: e.wait_ge(s[sem], v))
            return v

        def dyn_dma(q, out_fn, sem="ld"):
            v = P.inc(sem, 16)
            def run(e, s, out_fn=out_fn, sem=sem):
                o, i = out_fn(e)
                e.dma_start(out=o, in_=i).then_inc(s[sem], 16)
            P.on(q, run)
            return v

        def wait(key, sem, v):
            if v:
                P.on(key, lambda e, s, sem=sem, v=v: e.wait_ge(s[sem], v))

        def op(key, fn, sem=None, n=1):
            if sem is None:
                P.on(key, lambda e, s, fn=fn: fn(e))
                return None
            v = P.inc(sem, n)
            P.on(key, lambda e, s, fn=fn, sem=sem, n=n: fn(e).then_inc(s[sem], n))
            return v

        def vtrip(v=None):
            if v is None:
                v = P.val("ve")
            wait("vector", "ve", v)

        def ag(in_ap, out_ap):
            if no_cc:
                return 0
            v = P.inc("cc", 1)
            P.on("gpsimd", lambda e, s, in_ap=in_ap, out_ap=out_ap: e.collective_compute(
                "AllGather", OP.bypass, ins=[in_ap], outs=[out_ap],
                replica_groups=[list(range(NC))]).then_inc(s["cc"], 1))
            return v

        IDENT = alloc("IDENT", [128, 128], BF16)
        IDENTF = alloc("IDENTF", [128, 128], F32)
        ONESF = alloc("ONESF", [1, 128], F32)
        ONESB = alloc("ONESB", [1, 128], BF16)
        ONESA = alloc("ONESA", [128, 128], BF16)
        MASKT = alloc("MASKT", [128, 128], F32)
        MASKTT = alloc("MASKTT", [128, 128], F32)
        ZC32 = alloc("ZC32", [128, 96], BF16)
        COLS = alloc("COLS", [128, 16], F32)
        SXO = alloc("SXO", [128, NTC], F32)
        SACOL = alloc("SACOL", [128, NTC], F32)

        def transpose_strip(blk_in, blk_out, n):
            wait("tensor", "pz", P.val("pz"))
            wait("vector", "st", P.val("st"))
            ev = []
            nb = (n + 7) // 8
            for j in range(nb):
                boff = (j % 2) * 1024
                if j >= 2:
                    wait("tensor", "pz", ev[j - 2])
                g0, g1 = 8 * j, min(8 * j + 8, n)
                tv = None
                for g in range(g0, g1):
                    src_ap = blk_in(g)
                    slot = boff + 128 * (g - g0)
                    tv = op("tensor", lambda e, src_ap=src_ap, slot=slot: e.transpose(
                        PSB[:, slot:slot + 128], src_ap, IDENT[:]),
                        sem=("mm" if g == g1 - 1 else None))
                wait("vector", "mm", tv)
                pv = None
                for g in range(g0, g1):
                    dst_ap = blk_out(g)
                    slot = boff + 128 * (g - g0)
                    pv = op("vector", lambda e, dst_ap=dst_ap, slot=slot: e.tensor_copy(
                        dst_ap, PSB[:, slot:slot + 128]),
                        sem=("pz" if g == g1 - 1 else None))
                ev.append(pv)
            return ev[-1]

        for dst, src in [(IDENT[:], ident_in), (IDENTF[:], identf_in),
                         (ONESF[:], ones_in), (ONESB[:], onesb_in), (ONESA[:], onesa_in),
                         (MASKT[:], mtok_in), (MASKTT[:], mtt_in), (ZC32[:], zc32_in)]:
            dma("sync", dst, src)
        v0 = P.val("ld")
        wait("sync", "ld", v0)
        for k in ("vector", "tensor", "scalar"):
            wait(k, "ld", v0)

        est_w = contextlib.ExitStack()
        _alw = (lambda *a: est_w.enter_context(sbt(*a))) if not flat else alloc
        WQKVT = _alw("WQKVT", [128, 32 * 1536], BF16)
        SWB = _alw("SWB", [128, 1536], F32)

        with contextlib.ExitStack() as ph:
            al = (lambda *a: ph.enter_context(sbt(*a))) if not flat else alloc
            BIGA = al("p1_BIGA", [128, 4096], F32)
            BIGB = al("p1_BIGB", [128, 4096], F32)
            BFA = al("p1_BFA", [128, 4096], BF16)
            BFB = al("p1_BFB", [128, 4096], BF16)
            SWROW = al("p1_SWROW", [1, 1536], F32)
            SWOR = al("p1_SWOR", [1, 512], F32)

            xv = x_own.rearrange("(c p) d -> c p d", p=128)
            for tc in range(NTC):
                if tc > 0:
                    wait("sync", "ve", P.val("ve"))
                vl = dma("sync", BIGA[:], xv[tc])
                wait("vector", "ld", vl)
                vtrip(op("vector", lambda e: e.reduce_max(COLS[:, 0:1], BIGA[:], axis=AX.X, apply_absolute_value=True), sem="ve"))
                vtrip(op("vector", lambda e, tc=tc: e.tensor_scalar(SXO[:, tc:tc + 1], COLS[:, 0:1], 1.0 / 127.0, 1e-8, OP.mult, OP.max), sem="ve"))
                vtrip(op("vector", lambda e, tc=tc: e.reciprocal(COLS[:, 1:2], SXO[:, tc:tc + 1]), sem="ve"))
                op("vector", lambda e: e.tensor_scalar(BIGB[:], BIGA[:], COLS[:, 1:2], MAGIC, OP.mult, OP.add))
                vi = op("vector", lambda e: e.tensor_scalar(BFA[:], BIGB[:], -MAGIC, None, OP.add), sem="ve")
                if "fqraw" in tap_t:
                    wait("sync", "ve", vi)
                    dma("sync", tap_t["fqraw"][tc], BFA[:], sem="st")
                    dma("sync", tap_t["cols"][tc], COLS[:], sem="st")
                wait("tensor", "ve", vi)
                transpose_strip(lambda g: BFA[:, 128 * g:128 * (g + 1)],
                                lambda g: BFB[:, 128 * g:128 * (g + 1)], 32)
                wait("sync", "pz", P.val("pz"))
                dma("sync", agx_in[tc], BFB[:], sem="st")
                if "xint" in tap_t:
                    dma("sync", tap_t["xint"][tc], BFB[:], sem="st")
            wait("sync", "ve", P.val("ve"))
            dma("sync", agsx_in[:], SXO[:], sem="st")
            wait("gpsimd", "st", P.val("st"))
            cc_x = ag(agx_in[:], agx_out[:])
            cc_sx = ag(agsx_in[:], agsx_out[:])

            for wi, k in (enumerate("qkvo") if build_upto >= 2 else []):
                wv = w_in[k].rearrange("(c p) d -> c p d", p=128)
                for oc in range(4):
                    wait("sync", "ve", P.val("ve"))
                    vl = dma("sync", BIGA[:], wv[oc])
                    wait("vector", "ld", vl)
                    vtrip(op("vector", lambda e: e.reduce_max(COLS[:, 0:1], BIGA[:], axis=AX.X, apply_absolute_value=True), sem="ve"))
                    vtrip(op("vector", lambda e: e.tensor_scalar(COLS[:, 2:3], COLS[:, 0:1], 1.0 / 7.0, 1e-8, OP.mult, OP.max), sem="ve"))
                    vtrip(op("vector", lambda e: e.reciprocal(COLS[:, 1:2], COLS[:, 2:3]), sem="ve"))
                    op("vector", lambda e: e.tensor_scalar(BIGB[:], BIGA[:], COLS[:, 1:2], MAGIC, OP.mult, OP.add))
                    vi = op("vector", lambda e: e.tensor_scalar(BFA[:], BIGB[:], -MAGIC, None, OP.add), sem="ve")
                    wait("tensor", "ve", vi)
                    tv = op("tensor", lambda e: e.transpose(PS[0:1, 0:128], COLS[:, 2:3], IDENTF[:]), sem="mm")
                    wait("vector", "mm", tv)
                    rdst = (SWROW[:, wi * 512 + oc * 128: wi * 512 + (oc + 1) * 128]
                            if k != "o" else SWOR[:, oc * 128:(oc + 1) * 128])
                    vz = op("vector", lambda e, rdst=rdst: e.tensor_copy(rdst, PS[0:1, 0:128]), sem="pz")
                    wait("tensor", "pz", vz)
                    if k != "o":
                        transpose_strip(lambda g: BFA[:, 128 * g:128 * (g + 1)],
                                        lambda g, wi=wi, oc=oc: WQKVT[:, g * 1536 + wi * 512 + oc * 128: g * 1536 + wi * 512 + (oc + 1) * 128], 32)
                    else:
                        transpose_strip(lambda g: BFA[:, 128 * g:128 * (g + 1)],
                                        lambda g: BFB[:, 128 * g:128 * (g + 1)], 32)
                        wait("sync", "pz", P.val("pz"))
                        for g in range(32):
                            dma("sync", agwo_in[g][:, oc * 128:(oc + 1) * 128], BFB[:, 128 * g:128 * (g + 1)], sem="st")
            cc_wo = cc_swo = 0
            if build_upto >= 2:
                wait("tensor", "pz", P.val("pz"))
                for j in range(3):
                    tv = op("tensor", lambda e, j=j: e.matmul(bank(0), ONESF[:], SWROW[:, j * 512:(j + 1) * 512], start=True, stop=True), sem="mm")
                    wait("vector", "mm", tv)
                    vz = op("vector", lambda e, j=j: e.tensor_copy(SWB[:, j * 512:(j + 1) * 512], bank(0)), sem="pz")
                    wait("tensor", "pz", vz)
                wait("sync", "pz", P.val("pz"))
                dma("sync", agswo_in[:], SWOR[:], sem="st")
                wait("gpsimd", "st", P.val("st"))
                cc_wo = ag(agwo_in[:], agwo_out[:])
                cc_swo = ag(agswo_in[:], agswo_out[:])

        if build_upto >= 3:
            with contextlib.ExitStack() as ph:
                al = (lambda *a: ph.enter_context(sbt(*a))) if not flat else alloc
                E64 = al("p2b_E64", [64, 64 * 128], BF16)
                DQB = al("p2b_BIGB", [128, 2048], F32)
                DQC = al("p2b_BIGC", [128, 512], F32)
                QBY = al("p2b_QBY", [128, 2048], U8)
                QHI = al("p2b_QHI", [128, 2048], U8)
                WDQ = al("p2b_WDQ", [128, NCHP * 128], BF16)
                SGROWS = al("p2b_SGROWS", [64, NCHP * 128], BF16)
                SGF = al("p2b_SGF", [32, NCHP * 128], F32)
                ZROW = al("p2b_ZROW", [32, NCHP * 64], F32)
                DZN = al("p2b_DZN", [NCHP, 2048], U8)
                DZF = al("p2b_DZF", [NCHP, H], BF16)
                DSB = al("p2b_DSB", [NCHP, H], BF16)
                DZB = al("p2b_DZB", [NCHP, H], BF16)
                dma("sync", E64[:], e64_in)
                wait("vector", "ld", P.val("ld"))
                wait("tensor", "ld", P.val("ld"))

                dma("gpsimd", SGROWS[0:32, :], g_sc)
                dma("gpsimd", SGROWS[32:64, :], u_sc)
                v_srow = P.val("ld3")

                for m, (qz, scs) in enumerate([(g_qz, g_sc), (u_qz, u_sc)]):
                    wait("sync", "ve", P.val("ve"))
                    wait("vector", "st", P.val("st"))
                    vl = dma("sync", QBY[0:32, 0:704], qz)
                    vl2 = dma("sync", SGF[:], scs)
                    wait("vector", "ld", max(vl, vl2))
                    op("vector", lambda e: e.tensor_scalar(QHI[0:32, 0:704], QBY[0:32, 0:704], 4, None, OP.logical_shift_right))
                    op("vector", lambda e: e.scalar_tensor_tensor(ZROW[:], QHI[0:32, 0:704], -16.0, QBY[0:32, 0:704], OP.mult, OP.add))
                    last = None
                    for pos in range(8):
                        byi, half = AWQPOS[pos]
                        def fn(e, byi=byi, half=half, pos=pos):
                            srcap = ZROW[:] if half == 0 else QHI[0:32, 0:704]
                            srcs = srcap.rearrange("p (c b) -> p c b", b=4)[:, :, byi]
                            dsts = WDQ[0:32, :].rearrange("p (c j) -> p c j", j=8)[:, :, pos]
                            ssrc = SGF[:].rearrange("p (c j) -> p c j", j=8)[:, :, pos]
                            return e.scalar_tensor_tensor(dsts, srcs, -1.0, ssrc, OP.mult, OP.mult)
                        last = op("vector", fn, sem="ve")
                    wait("sync", "ve", last)
                    off = 0 if m == 0 else ZSZ
                    dma("sync", agz_in[:, off:off + ZSZ].rearrange("o (p f) -> (o p) f", p=32), WDQ[0:32, :], sem="st")

                wait("tensor", "ld3", v_srow)
                for m, qw in enumerate([g_qw, u_qw]):
                    qwv = qw.rearrange("(g p) c -> g p c", p=128)
                    for g in range(32):
                        wait("sync", "ve", P.val("ve"))
                        wait("vector", "st", P.val("st"))
                        vl = dma("sync", QBY[:, 0:704], qwv[g])
                        row = m * 32 + g
                        wait("tensor", "ve", P.val("ve"))
                        tvl = None
                        for wnd, wid in enumerate([512, 512, 384]):
                            tvl = op("tensor", lambda e, wnd=wnd, wid=wid, row=row: e.matmul(
                                PS[:, 512 * wnd:512 * wnd + wid], E64[:, row * 128:(row + 1) * 128],
                                SGROWS[:, 512 * wnd:512 * wnd + wid],
                                start=True, stop=True), sem="mm")
                        wait("vector", "mm", tvl)
                        wait("vector", "ld", vl)
                        op("vector", lambda e: e.tensor_scalar(QHI[:, 0:704], QBY[:, 0:704], 4, None, OP.logical_shift_right))
                        op("vector", lambda e: e.scalar_tensor_tensor(DQB[:, 0:704], QHI[:, 0:704], -16.0, QBY[:, 0:704], OP.mult, OP.add))
                        last = None
                        for pos in range(8):
                            byi, half = AWQPOS[pos]
                            def fn(e, byi=byi, half=half, pos=pos):
                                srcap = DQB[:, 0:704] if half == 0 else QHI[:, 0:704]
                                srcs = srcap.rearrange("p (c b) -> p c b", b=4)[:, :, byi]
                                dsts = WDQ[:].rearrange("p (c j) -> p c j", j=8)[:, :, pos]
                                sbs = PS[:, 0:1408].rearrange("p (c j) -> p c j", j=8)[:, :, pos]
                                return e.tensor_tensor(dsts, srcs, sbs, OP.mult)
                            last = op("vector", fn, sem="ve")
                        wait("sync", "ve", last)
                        agio = agg_in if m == 0 else agu_in
                        for lc in range(NCHP):
                            dst = agio[:, lc * (128 * H): (lc + 1) * (128 * H)]
                            dstv = dst.rearrange("o (p f) -> (o p) f", p=128)
                            dma("sync", dstv[:, g * 128:(g + 1) * 128], WDQ[:, lc * 128:(lc + 1) * 128], sem="st")

                wait("sync", "ve", P.val("ve"))
                vl = dma("sync", DZN[:], d_qz)
                vl2 = dma("gpsimd", DSB[:], d_sc)
                wait("vector", "ld", vl)
                wait("vector", "ld3", vl2)
                op("vector", lambda e: e.tensor_scalar(QHI[0:NCHP, 0:2048], DZN[:], 4, None, OP.logical_shift_right))
                op("vector", lambda e: e.scalar_tensor_tensor(DQB[0:NCHP, 0:2048], QHI[0:NCHP, 0:2048], -16.0, DZN[:], OP.mult, OP.add))
                for pos in range(8):
                    byi, half = AWQPOS[pos]
                    def fn(e, byi=byi, half=half, pos=pos):
                        srcap = DQB[0:NCHP, 0:2048] if half == 0 else QHI[0:NCHP, 0:2048]
                        srcs = srcap.rearrange("p (c b) -> p c b", b=4)[:, :, byi]
                        dsts = DZF[:].rearrange("p (c j) -> p c j", j=8)[:, :, pos]
                        return e.tensor_copy(dsts, srcs)
                    op("vector", fn)
                vzf = op("vector", lambda e: e.tensor_tensor(DZB[:], DZF[:], DSB[:], OP.mult), sem="ve")
                dqwv = d_qw.rearrange("(l p) c -> l p c", p=128)
                wait("tensor", "ve", vzf)
                for lc in range(NCHP):
                    wait("sync", "ve", P.val("ve"))
                    vl = dma("sync", QBY[:, 0:2048], dqwv[lc])
                    wait("vector", "ld", vl)
                    op("vector", lambda e: e.tensor_scalar(QHI[:, 0:2048], QBY[:, 0:2048], 4, None, OP.logical_shift_right))
                    op("vector", lambda e: e.scalar_tensor_tensor(DQB[:, 0:2048], QHI[:, 0:2048], -16.0, QBY[:, 0:2048], OP.mult, OP.add))
                    for w8 in range(8):
                        wait("tensor", "ve", P.val("ve"))
                        wait("vector", "st", P.val("st"))
                        op("tensor", lambda e, w8=w8, lc=lc: e.matmul(bank(0), E64[0:NCHP, lc * 128:(lc + 1) * 128], DSB[:, w8 * 512:(w8 + 1) * 512], start=True, stop=True))
                        tv2 = op("tensor", lambda e, w8=w8, lc=lc: e.matmul(bank(1), E64[0:NCHP, lc * 128:(lc + 1) * 128], DZB[:, w8 * 512:(w8 + 1) * 512], start=True, stop=True), sem="mm")
                        wait("vector", "mm", tv2)
                        for pos in range(8):
                            byi, half = AWQPOS[pos]
                            def fn(e, byi=byi, half=half, pos=pos, w8=w8):
                                srcap = DQB[:, 0:2048] if half == 0 else QHI[:, 0:2048]
                                srcs = srcap.rearrange("p (c b) -> p c b", b=4)[:, w8 * 64:(w8 + 1) * 64, byi]
                                sbs = bank(0).rearrange("p (c j) -> p c j", j=8)[:, :, pos]
                                tmp = DQC[:, 0:512].rearrange("p (c j) -> p c j", j=8)[:, :, pos]
                                return e.tensor_tensor(tmp, srcs, sbs, OP.mult)
                            op("vector", fn)
                        op("vector", lambda e: e.tensor_tensor(DQC[:, 0:512], DQC[:, 0:512], bank(1), OP.subtract))
                        vv = op("vector", lambda e: e.tensor_copy(WDQ[:, 0:512], DQC[:, 0:512]), sem="ve")
                        wait("sync", "ve", vv)
                        off = w8 * (128 * NCHP * 512)
                        dstv = agd_in[:, off:off + 128 * NCHP * 512].rearrange("o (p f) -> (o p) f", p=128)
                        dma("sync", dstv[:, lc * 512:(lc + 1) * 512], WDQ[:, 0:512], sem="st")
                wait("gpsimd", "st", P.val("st"))
                cc_g = ag(agg_in[:], agg_out[:])
                cc_u = ag(agu_in[:], agu_out[:])
                cc_z = ag(agz_in[:], agz_out[:])
                cc_f = ag(agd_in[:], agd_out[:])
                if "agf" in tap_t:
                    dma("sync", tap_t["agf"][:, 0:GSZ], agg_in[:], sem="st")
                    dma("sync", tap_t["agf"][:, GSZ:2 * GSZ], agu_in[:], sem="st")
                    dma("sync", tap_t["agf"][:, OFF_ZG:OFF_ZG + 2 * ZSZ], agz_in[:], sem="st")
                    dma("sync", tap_t["agf"][:, OFF_D:OFF_D + DSZ], agd_in[:], sem="st")
                if build_upto < 3:
                    pass

            if build_upto >= 4:
                with contextlib.ExitStack() as ph:
                    al = (lambda *a: ph.enter_context(sbt(*a))) if not flat else alloc
                    COSB = al("p3_COSB", [128, 16 * 128], F32)
                    SINB = al("p3_SINB", [128, 16 * 128], F32)
                    SXALL = al("p3_SXALL", [128, GTC], F32)
                    XTT = Buf(al("p3_XTT", [128, 2 * 4096], BF16), 4096)
                    QKVE = al("p3_QKVE", [128, 1536], F32)
                    ROPC = al("p3_ROPC", [128, 512], F32)
                    ROPS = al("p3_ROPS", [128, 512], F32)
                    QKI = al("p3_QKI", [128, 1024], BF16)
                    INVC = al("p3_INVC", [128, 16], F32)
                    QTS = al("p3_QTS", [128, 512], BF16)
                    KTS = al("p3_KTS", [128, 512], BF16)
                    ESC = al("p3_ESC", [128, 512], BF16)
                    QKB = al("p3_BIGB", [128, 1536], F32)
                    QKC = al("p3_BIGC", [128, 2048], F32)
                    VFB = al("p3_BFB", [128, 512], BF16)

                    cosv = cos_in.rearrange("(c p) d -> c p d", p=128)
                    sinv = sin_in.rearrange("(c p) d -> c p d", p=128)
                    for scn in range(16):
                        dma("sync", COSB[:, scn * 128:(scn + 1) * 128], cosv[scn])
                        dma("sync", SINB[:, scn * 128:(scn + 1) * 128], sinv[scn])
                    wait("sync", "cc", cc_sx)
                    for r in range(NC):
                        dma("sync", SXALL[:, r * NTC:(r + 1) * NTC], agsxv[r])
                    wait("vector", "ld", P.val("ld"))
                    wait("sync", "cc", cc_x)
                    mm_mark = {}
                    for tcg in range(GTC):
                        buf = tcg % 2
                        bk = (tcg % 2) * 3
                        wait("vector", "st", P.val("st"))
                        if tcg >= 2:
                            wait("sync", "mm", mm_mark[tcg - 2])
                        vx = dma("sync", XTT[buf], agxv[tcg // NTC][tcg % NTC])
                        if tcg >= 2:
                            wait("tensor", "ve", P.val("ve"))
                        wait("tensor", "ld", vx)
                        tmv = None
                        for g in range(32):
                            for j3 in range(3):
                                sp = (g == 31 and j3 == 2)
                                tmv2 = op("tensor", lambda e, g=g, j3=j3, bk=bk, buf=buf:
                                          e.matmul(bank(bk + j3), XTT[buf][:, 128 * g:128 * (g + 1)],
                                                   WQKVT[:, g * 1536 + j3 * 512: g * 1536 + (j3 + 1) * 512],
                                                   start=(g == 0), stop=(g == 31)),
                                          sem=("mm" if sp else None))
                                if sp:
                                    tmv = tmv2
                        mm_mark[tcg] = tmv
                        wait("vector", "mm", tmv)
                        scn = tcg % 16
                        for j3 in range(3):
                            op("vector", lambda e, j3=j3, bk=bk, tcg=tcg: e.tensor_scalar(
                                QKB[:, j3 * 512:(j3 + 1) * 512], bank(bk + j3),
                                SXALL[:, tcg:tcg + 1], None, OP.mult))
                        op("vector", lambda e: e.tensor_tensor(QKVE[:], QKB[:, 0:1536], SWB[:], OP.mult), sem="ve")
                        for base in (0, 512):
                            for hl in range(HPC):
                                c0, h0, cs = base + hl * 128, hl * 128, scn * 128
                                op("vector", lambda e, c0=c0, h0=h0, cs=cs: e.tensor_tensor(
                                    ROPC[:, h0:h0 + 128], QKVE[:, c0:c0 + 128], COSB[:, cs:cs + 128], OP.mult))
                                op("vector", lambda e, c0=c0, h0=h0, cs=cs: e.tensor_tensor(
                                    ROPS[:, h0:h0 + 64], QKVE[:, c0 + 64:c0 + 128], SINB[:, cs:cs + 64], OP.mult))
                                op("vector", lambda e, c0=c0, h0=h0, cs=cs: e.tensor_tensor(
                                    ROPS[:, h0 + 64:h0 + 128], QKVE[:, c0:c0 + 64], SINB[:, cs + 64:cs + 128], OP.mult))
                            for hl in range(HPC):
                                h0 = hl * 128
                                op("vector", lambda e, h0=h0, base=base: e.tensor_tensor(
                                    QKC[:, base + h0:base + h0 + 64], ROPC[:, h0:h0 + 64], ROPS[:, h0:h0 + 64], OP.subtract))
                                op("vector", lambda e, h0=h0, base=base: e.tensor_tensor(
                                    QKC[:, base + h0 + 64:base + h0 + 128], ROPC[:, h0 + 64:h0 + 128], ROPS[:, h0 + 64:h0 + 128], OP.add))
                        srcs12 = ([("qk", hl * 128) for hl in range(HPC)] +
                                  [("qk", 512 + hl * 128) for hl in range(HPC)] +
                                  [("v", 1024 + hl * 128) for hl in range(HPC)])
                        rv = None
                        for ri, (kind, c0) in enumerate(srcs12):
                            srcb = QKC if kind == "qk" else QKVE
                            rv = op("vector", lambda e, ri=ri, c0=c0, srcb=srcb: e.reduce_max(
                                COLS[:, ri:ri + 1], srcb[:, c0:c0 + 128], axis=AX.X, apply_absolute_value=True),
                                sem=("ve" if ri == 11 else None))
                        vtrip(rv)
                        for ri in range(4):
                            op("vector", lambda e, ri=ri: e.tensor_scalar(
                                INVC[:, 12 + ri:13 + ri], COLS[:, ri:ri + 1], INVSQ / 127.0, 1e-8 * INVSQ, OP.mult, OP.max))
                        vsc12 = None
                        for ri in range(12):
                            vsc12 = op("vector", lambda e, ri=ri: e.tensor_scalar(COLS[:, ri:ri + 1], COLS[:, ri:ri + 1], 1.0 / 127.0, 1e-8, OP.mult, OP.max),
                                       sem=("ve" if ri == 11 else None))
                        vtrip(vsc12)
                        rvv = None
                        for ri in range(12):
                            rvv = op("vector", lambda e, ri=ri: e.reciprocal(INVC[:, ri:ri + 1], COLS[:, ri:ri + 1]),
                                     sem=("ve" if ri == 11 else None))
                        vtrip(rvv)
                        for ri, (kind, c0) in enumerate(srcs12):
                            srcb = QKC if kind == "qk" else QKVE
                            op("vector", lambda e, ri=ri, c0=c0, srcb=srcb: e.tensor_scalar(
                                QKC[:, 1536:1664], srcb[:, c0:c0 + 128], INVC[:, ri:ri + 1], MAGIC, OP.mult, OP.add))
                            dsti = QKI[:, c0:c0 + 128] if c0 < 1024 else QKC[:, 1664:1792]
                            op("vector", lambda e, dsti=dsti: e.tensor_scalar(dsti, QKC[:, 1536:1664], -MAGIC, None, OP.add))
                            if ri < 8:
                                dst = QTS if ri < 4 else KTS
                                scl = INVC[:, 12 + ri:13 + ri] if ri < 4 else COLS[:, ri:ri + 1]
                                hl = ri % 4
                                op("vector", lambda e, c0=c0, dst=dst, hl=hl, scl=scl: e.tensor_scalar(
                                    dst[:, hl * 128:(hl + 1) * 128], QKI[:, c0:c0 + 128], scl, None, OP.mult))
                            else:
                                hl = ri % 4
                                op("vector", lambda e, hl=hl, ri=ri: e.tensor_scalar(
                                    VFB[:, hl * 128:(hl + 1) * 128], QKC[:, 1664:1792], COLS[:, ri:ri + 1], None, OP.mult))
                        vqk = op("vector", lambda e: e.tensor_copy(COLS[:, 15:16], COLS[:, 15:16]), sem="ve")
                        wait("sync", "ve", vqk)
                        dma("sync", vt_d[tcg * 128:(tcg + 1) * 128, :], VFB[:], sem="st")
                        if "vt" in tap_t:
                            dma("sync", tap_t["vt"][tcg * 128:(tcg + 1) * 128, :], VFB[:], sem="st")
                        wait("tensor", "ve", vqk)
                        for srcb, dstd, tapn in ((QTS, qt_d, "qt"), (KTS, kt_d, "kt")):
                            transpose_strip(lambda g, srcb=srcb: srcb[:, 128 * g:128 * (g + 1)],
                                            lambda g: ESC[:, g * 128:(g + 1) * 128], HPC)
                            wait("sync", "pz", P.val("pz"))
                            dsl = dstd[:, :, tcg * 128:(tcg + 1) * 128].rearrange("h p t -> p h t")
                            sslv = ESC[:].rearrange("p (h t) -> p h t", h=HPC)
                            dma("sync", dsl, sslv, sem="st")
                            if tapn in tap_t:
                                dma("sync", tap_t[tapn][:, :, tcg * 128:(tcg + 1) * 128].rearrange("h p t -> p h t"), sslv, sem="st")

        if not flat:
            est_w.close()

        if build_upto >= 5:
            with contextlib.ExitStack() as ph:
                al = (lambda *a: ph.enter_context(sbt(*a))) if not flat else alloc
                QTB = Buf(al("p4_QTB", [128, 2 * S], BF16), S)
                KTB = Buf(al("p4_KTB", [128, 2 * S], BF16), S)
                VB = Buf(al("p4_VB", [128, 2 * 2048], BF16), 2048)
                ESCT = al("p4_ESCT", [128, 2048], BF16)
                E1 = Buf(al("p4_E1", [128, 3 * 512], F32), 512)
                ESC3 = Buf(al("p4_ESC3", [128, 3 * 512], BF16), 512)
                PINT = Buf(al("p4_PINT", [128, 3 * 512], BF16), 512)
                MCOL = al("p4_MCOL", [128, 16], F32)
                SCOL = al("p4_SCOL", [128, 16], F32)
                MNEG = al("p4_MNEG", [128, 1], F32)
                MRSB = al("p4_MRSB", [1, 2048], F32)
                MB = al("p4_MB", [128, 2048], F32)
                ATE = al("p4_ATE", [128, 2048], BF16)
                STILE = al("p4_STILE", [128, 256], F32)

                wait("sync", "st", P.val("st"))
                bh_list = [(b, hl) for b in range(4) for hl in range(HPC)]
                ld_bh = {}

                def load_bh(i):
                    b, hl = bh_list[i]
                    bf = i % 2
                    v1 = dma("sync", QTB[bf], qt_d[hl][:, b * S:(b + 1) * S])
                    v2 = dma("sync", KTB[bf], kt_d[hl][:, b * S:(b + 1) * S])
                    v3 = 0
                    for tkc in range(16):
                        v3 = dma("sync", VB[bf][:, tkc * 128:(tkc + 1) * 128],
                                 vt_d[b * S + tkc * 128: b * S + (tkc + 1) * 128, hl * 128:(hl + 1) * 128])
                    ld_bh[i] = max(v1, v2, v3)
                load_bh(0)
                st_free = [None, None]
                attn_free = [None, None]
                for i, (b, hl) in enumerate(bh_list):
                    bf = i % 2
                    if i + 1 < len(bh_list):
                        load_bh(i + 1)
                    wait("vector", "st", P.val("st"))
                    wait("vector", "mm", P.val("mm"))
                    wait("tensor", "ld", ld_bh[i])
                    for tqc in range(16):
                        wid_all = (tqc + 1) * 128
                        if tqc > 0 or i > 0:
                            wait("tensor", "sc", P.val("sc"))
                        nw = (wid_all + 511) // 512
                        tv = None
                        for w in range(nw):
                            wd = min(512, wid_all - 512 * w)
                            tv = op("tensor", lambda e, tqc=tqc, w=w, wd=wd, bf=bf:
                                    e.matmul(PS[:, 512 * w:512 * w + wd],
                                             QTB[bf][:, tqc * 128:(tqc + 1) * 128],
                                             KTB[bf][:, 512 * w:512 * w + wd],
                                             start=True, stop=True),
                                    sem=("mm" if w == nw - 1 else None))
                        wait("vector", "mm", tv)
                        offd = tqc * 128
                        op("vector", lambda e, offd=offd: e.tensor_tensor(
                            PS[:, offd:offd + 128], PS[:, offd:offd + 128], MASKT[:], OP.add))
                        vtrip(op("vector", lambda e, wid_all=wid_all, tqc=tqc: e.reduce_max(
                            MCOL[:, tqc:tqc + 1], PS[:, 0:wid_all], axis=AX.X), sem="ve"))
                        vng = op("vector", lambda e, tqc=tqc: e.tensor_scalar(
                            MNEG[:], MCOL[:, tqc:tqc + 1], -1.0, None, OP.mult), sem="ve")
                        wait("scalar", "ve", vng)
                        op("scalar", lambda e, wid_all=wid_all, tqc=tqc: e.activation(
                            ESCT[:, 0:wid_all], PS[:, 0:wid_all], AF.Exp,
                            bias=MNEG[:], accum_out=SCOL[:, tqc:tqc + 1]), sem="sc")
                    vsc = P.val("sc")
                    wait("vector", "sc", vsc)
                    for jd in range(2):
                        op("vector", lambda e, b=b, hl=hl, jd=jd: e.tensor_copy(
                            STILE[:, (2 * b + jd) * 32 + hl * 8: (2 * b + jd) * 32 + hl * 8 + 8],
                            SCOL[:, jd * 8:(jd + 1) * 8]))
                    vstok = op("vector", lambda e: e.tensor_copy(COLS[:, 15:16], COLS[:, 15:16]), sem="ve")
                    wait("tensor", "sc", vsc)
                    for tqc in range(16):
                        sl = tqc % 4
                        tv = op("tensor", lambda e, tqc=tqc, sl=sl: e.transpose(
                            PS[0:1, 1024 + sl * 128:1024 + sl * 128 + 128], MCOL[:, tqc:tqc + 1], IDENTF[:]), sem="mm")
                        wait("vector", "mm", tv)
                        vz = op("vector", lambda e, tqc=tqc, sl=sl: e.tensor_copy(
                            MRSB[:, tqc * 128:(tqc + 1) * 128], PS[0:1, 1024 + sl * 128:1024 + sl * 128 + 128]), sem="pz")
                        wait("tensor", "pz", vz)
                    for w in range(4):
                        tv = op("tensor", lambda e, w=w: e.matmul(
                            PS[:, 1024:1536], ONESF[:], MRSB[:, w * 512:(w + 1) * 512], start=True, stop=True), sem="mm")
                        wait("vector", "mm", tv)
                        vz = op("vector", lambda e, w=w: e.tensor_copy(MB[:, w * 512:(w + 1) * 512], PS[:, 1024:1536]), sem="pz")
                        wait("tensor", "pz", vz)
                    for w in range(4):
                        abk = 4 + (w % 2)
                        if attn_free[w % 2] is not None:
                            wait("tensor", "ve", attn_free[w % 2])
                        ntk = 4 * w + 4
                        for tkc in range(ntk):
                            sl = tkc % 2
                            sbk = 2 + sl
                            if st_free[sl] is not None:
                                wait("tensor", "ve", st_free[sl])
                            off = max(0, tkc * 128 - 512 * w)
                            tv = op("tensor", lambda e, tkc=tkc, w=w, off=off, sbk=sbk, bf=bf:
                                    e.matmul(PS[:, 512 * sbk + off:512 * sbk + 512],
                                             KTB[bf][:, tkc * 128:(tkc + 1) * 128],
                                             QTB[bf][:, 512 * w + off:512 * w + 512],
                                             start=True, stop=True), sem="mm")
                            wait("vector", "mm", tv)
                            if tkc * 128 >= 512 * w:
                                op("vector", lambda e, sbk=sbk, off=off: e.tensor_tensor(
                                    PS[:, 512 * sbk + off:512 * sbk + off + 128],
                                    PS[:, 512 * sbk + off:512 * sbk + off + 128], MASKTT[:], OP.add))
                            vs = op("vector", lambda e, sbk=sbk, off=off, w=w, sl=sl: e.scalar_tensor_tensor(
                                E1[sl][:, off:512], PS[:, 512 * sbk + off:512 * sbk + 512], LN127,
                                MB[:, 512 * w + off:512 * w + 512], OP.add, OP.subtract), sem="ve")
                            st_free[sl] = vs
                            wait("scalar", "ve", vs)
                            sv = op("scalar", lambda e, sl=sl, off=off: e.activation(
                                ESC3[sl][:, off:512], E1[sl][:, off:512], AF.Exp), sem="sc")
                            wait("vector", "sc", sv)
                            rv = op("vector", lambda e, sl=sl, off=off: e.tensor_scalar(
                                PINT[sl][:, off:512], ESC3[sl][:, off:512], MAGIC, -MAGIC, OP.add, OP.add), sem="ve")
                            wait("tensor", "ve", rv)
                            op("tensor", lambda e, tkc=tkc, sl=sl, off=off, abk=abk, bf=bf, ntk=ntk:
                               e.matmul(PS[:, 512 * abk + off:512 * abk + 512],
                                        VB[bf][:, tkc * 128:(tkc + 1) * 128],
                                        PINT[sl][:, off:512],
                                        start=(tkc == 0), stop=(tkc == ntk - 1)),
                               sem=("mm" if tkc == ntk - 1 else None))
                        tva = P.val("mm")
                        wait("vector", "mm", tva)
                        av = op("vector", lambda e, w=w, abk=abk: e.tensor_copy(
                            ATE[:, 512 * w:512 * (w + 1)], PS[:, 512 * abk:512 * abk + 512]), sem="ve")
                        attn_free[w % 2] = av
                    wait("sync", "ve", P.val("ve"))
                    for jd in range(2):
                        dma("sync", agai[2 * b + jd][hl * 128:(hl + 1) * 128, :],
                            ATE[:, jd * TO:(jd + 1) * TO], sem="st")
                    if "attn" in tap_t:
                        dma("sync", tap_t["attn"][hl * 128:(hl + 1) * 128, b * S:(b + 1) * S], ATE[:], sem="st")
                wait("sync", "ve", P.val("ve"))
                for d in range(NC):
                    dma("sync", agsi[d], STILE[:, d * 32:(d + 1) * 32], sem="st")
                if "ssum" in tap_t:
                    dma("sync", tap_t["ssum"][:], STILE[:], sem="st")
                wait("gpsimd", "st", P.val("st"))
                cc_a = ag(aga_in[:], aga_out[:])
                cc_s = ag(ags_in[:], ags_out[:])

        if build_upto >= 6:
            with contextlib.ExitStack() as ph:
                al = (lambda *a: ph.enter_context(sbt(*a))) if not flat else alloc
                P5A = al("p5_BIGA", [128, 4096], F32)
                P5B = al("p5_BIGB", [128, 4096], F32)
                P5FA = al("p5_BFA", [128, 4096], BF16)
                P5FB = al("p5_BFB", [128, 4096], BF16)
                INVS = al("p5_INVS", [128, 32 * NTC], F32)
                SWOB = al("p5_SWOB", [128, H], F32)
                SWR1 = al("p5_SWR1", [1, 512], F32)
                AITP = Buf(al("p5_AITP", [128, 2 * 4096], BF16), 4096)
                WOT = Buf(al("p5_WOT", [128, 4 * 8192], BF16), 8192)
                WOE = al("p5_WOE", [128, 512], F32)
                O1A = al("p5_O1A", [128, 4096], F32)
                O1B = al("p5_O1B", [128, 4096], F32)

                wait("sync", "cc", P.val("cc"))
                for r in range(NC):
                    def mka(e, r=r):
                        pid = get_pid(e)
                        srcb = agav[r * NC:(r + 1) * NC][bass.ts(pid, 1)]
                        return (agal[r:r + 1], srcb)
                    dyn_dma("sync", mka)
                    def mks(e, r=r):
                        pid = get_pid(e)
                        srcb = agsv[r * NC:(r + 1) * NC][bass.ts(pid, 1)]
                        return (agsl[r:r + 1], srcb)
                    dyn_dma("sync", mks)
                v_blk = P.val("ld")
                wait("sync", "ld", v_blk)
                for r in range(NC):
                    dma("sync", INVS[:, r * 32:(r + 1) * 32], agsl[r])
                v_invs = P.val("ld")
                wait("vector", "ld", v_invs)
                op("vector", lambda e: e.tensor_scalar(INVS[:], INVS[:], 127.0, None, OP.mult))
                vtrip(op("vector", lambda e: e.reciprocal(INVS[:], INVS[:]), sem="ve"))
                for r in range(NC):
                    if r > 0:
                        wait("sync", "mm", P.val("mm"))
                    v_swo = dma("sync", SWR1[:], agswo_out[r:r + 1, :])
                    wait("tensor", "ld", v_swo)
                    tv = op("tensor", lambda e, r=r: e.matmul(bank(0), ONESF[:], SWR1[:], start=True, stop=True), sem="mm")
                    wait("vector", "mm", tv)
                    vz = op("vector", lambda e, r=r: e.tensor_copy(SWOB[:, r * 512:(r + 1) * 512], bank(0)), sem="pz")
                    wait("tensor", "pz", vz)
                for tc in range(NTC):
                    wait("sync", "mm", P.val("mm"))
                    wait("sync", "ve", P.val("ve"))
                    vas = 0
                    for r in range(NC):
                        for gl in range(HPC):
                            g = r * HPC + gl
                            vas = dma("sync", P5FA[:, g * 128:(g + 1) * 128],
                                      agal[r][gl * 128:(gl + 1) * 128, tc * 128:(tc + 1) * 128])
                    wait("tensor", "ld", vas)
                    wait("tensor", "ve", P.val("ve"))
                    transpose_strip(lambda g: P5FA[:, g * 128:(g + 1) * 128],
                                    lambda g: P5FB[:, g * 128:(g + 1) * 128], 32)
                    wait("vector", "pz", P.val("pz"))
                    for g in range(32):
                        r2, hl2 = g // HPC, g % HPC
                        ic = r2 * 32 + hl2 * 8 + tc
                        op("vector", lambda e, g=g, ic=ic: e.tensor_scalar(
                            P5A[:, g * 128:(g + 1) * 128], P5FB[:, g * 128:(g + 1) * 128],
                            INVS[:, ic:ic + 1], None, OP.mult))
                    vtrip(op("vector", lambda e: e.reduce_max(COLS[:, 4:5], P5A[:], axis=AX.X, apply_absolute_value=True), sem="ve"))
                    vtrip(op("vector", lambda e, tc=tc: e.tensor_scalar(SACOL[:, tc:tc + 1], COLS[:, 4:5], 1.0 / 127.0, 1e-8, OP.mult, OP.max), sem="ve"))
                    vtrip(op("vector", lambda e, tc=tc: e.reciprocal(COLS[:, 6:7], SACOL[:, tc:tc + 1]), sem="ve"))
                    op("vector", lambda e: e.tensor_scalar(P5B[:], P5A[:], COLS[:, 6:7], MAGIC, OP.mult, OP.add))
                    vi = op("vector", lambda e: e.tensor_scalar(P5FA[:], P5B[:], -MAGIC, None, OP.add), sem="ve")
                    wait("tensor", "ve", vi)
                    transpose_strip(lambda g: P5FA[:, 128 * g:128 * (g + 1)],
                                    lambda g: P5FB[:, g * 128:(g + 1) * 128], 32)
                    wait("sync", "pz", P.val("pz"))
                    dma("sync", ait_d[tc], P5FB[:], sem="st")
                    if "aint" in tap_t:
                        dma("sync", tap_t["aint"][tc], P5FB[:], sem="st")
                wait("sync", "cc", cc_wo)
                wo_mark = {}
                for tcp in range(4):
                    tc0, tc1 = 2 * tcp, 2 * tcp + 1
                    wait("sync", "mm", P.val("mm"))
                    va0 = dma("sync", AITP[0], ait_d[tc0])
                    va1 = dma("sync", AITP[1], ait_d[tc1])
                    wait("tensor", "ld", max(va0, va1))
                    for rg in range(4):
                        r0, r1 = 2 * rg, 2 * rg + 1
                        prev_mark = wo_mark.get("last")
                        if prev_mark:
                            wait("scalar", "mm", prev_mark)
                        woldv = {}
                        for q4 in range(4):
                            lv = 0
                            for ri, r in enumerate((r0, r1)):
                                dstw = WOT[q4][:, ri * 4096:(ri + 1) * 4096].rearrange("p (g f) -> p g f", g=8)
                                srcw = agwov[r][8 * q4:8 * (q4 + 1)].rearrange("g p f -> p g f")
                                lv = dma("scalar", dstw, srcw)
                            woldv[q4] = lv
                        wait("tensor", "ve", P.val("ve"))
                        two = None
                        for q4 in range(4):
                            wait("tensor", "ld2", woldv[q4])
                            for g8 in range(8):
                                g = q4 * 8 + g8
                                for ti in range(2):
                                    for ri in range(2):
                                        last = (g == 31 and ti == 1 and ri == 1)
                                        two2 = op("tensor", lambda e, g=g, g8=g8, ti=ti, ri=ri, q4=q4:
                                                  e.matmul(bank(2 * ti + ri),
                                                           AITP[ti][:, g * 128:(g + 1) * 128],
                                                           WOT[q4][:, ri * 4096 + g8 * 512: ri * 4096 + (g8 + 1) * 512],
                                                           start=(g == 0), stop=(g == 31)),
                                                  sem=("mm" if last else None))
                                        if last:
                                            two = two2
                        wo_mark["last"] = two
                        wait("vector", "mm", two)
                        for ti, tcx in enumerate((tc0, tc1)):
                            dstt = O1A if ti == 0 else O1B
                            for ri, r in enumerate((r0, r1)):
                                op("vector", lambda e, ti=ti, ri=ri, tcx=tcx: e.tensor_scalar(
                                    WOE[:], bank(2 * ti + ri), SACOL[:, tcx:tcx + 1], None, OP.mult))
                                op("vector", lambda e, dstt=dstt, r=r: e.tensor_tensor(
                                    dstt[:, r * 512:(r + 1) * 512], WOE[:], SWOB[:, r * 512:(r + 1) * 512], OP.mult))
                        op("vector", lambda e: e.tensor_copy(COLS[:, 15:16], COLS[:, 15:16]), sem="ve")
                    for ti, tcx in enumerate((tc0, tc1)):
                        srct = O1A if ti == 0 else O1B
                        wait("sync", "ve", P.val("ve"))
                        vl = dma("sync", P5A[:], xv[tcx])
                        wait("vector", "ld", vl)
                        if "o1" in tap_t:
                            vv1 = op("vector", lambda e, srct=srct: e.tensor_copy(P5B[:], srct[:]), sem="ve")
                            wait("sync", "ve", vv1)
                            dma("sync", tap_t["o1"][tcx], P5B[:], sem="st")
                        op("vector", lambda e, srct=srct: e.tensor_tensor(P5B[:], srct[:], P5A[:], OP.add))
                        vz = op("vector", lambda e: e.reduce_sum(COLS[:, 0:1], P5B[:], axis=AX.X), sem="ve")
                        vtrip(vz)
                        wait("scalar", "ve", vz)
                        sv = op("scalar", lambda e: e.activation(P5A[:], P5B[:], AF.Square, accum_out=COLS[:, 1:2]), sem="sc")
                        wait("vector", "sc", sv)
                        op("vector", lambda e: e.tensor_scalar(COLS[:, 2:3], COLS[:, 0:1], 1.0 / H, None, OP.mult))
                        vv9 = op("vector", lambda e: e.tensor_scalar(COLS[:, 3:4], COLS[:, 1:2], 1.0 / H, None, OP.mult), sem="ve")
                        vtrip(vv9)
                        vv9 = op("vector", lambda e: e.tensor_scalar(COLS[:, 4:5], COLS[:, 2:3], COLS[:, 2:3], None, OP.mult), sem="ve")
                        vtrip(vv9)
                        vz2 = op("vector", lambda e: e.tensor_scalar(COLS[:, 6:7], COLS[:, 3:4], COLS[:, 4:5], EPS, OP.subtract, OP.add), sem="ve")
                        vtrip(vz2)
                        wait("scalar", "ve", vz2)
                        sv2 = op("scalar", lambda e: e.activation(COLS[:, 7:8], COLS[:, 6:7], AF.Sqrt), sem="sc")
                        wait("vector", "sc", sv2)
                        vtrip(op("vector", lambda e: e.reciprocal(COLS[:, 8:9], COLS[:, 7:8]), sem="ve"))
                        vy = op("vector", lambda e: e.tensor_scalar(P5A[:], P5B[:], COLS[:, 2:3], COLS[:, 8:9], OP.subtract, OP.mult), sem="ve")
                        wait("sync", "ve", vy)
                        dma("sync", yf_d[tcx], P5A[:], sem="st")
                        if "y" in tap_t:
                            dma("sync", tap_t["y"][tcx], P5A[:], sem="st")
                        vyb = op("vector", lambda e: e.tensor_copy(P5FA[:], P5A[:]), sem="ve")
                        wait("tensor", "ve", vyb)
                        transpose_strip(lambda g: P5FA[:, 128 * g:128 * (g + 1)],
                                        lambda g: P5FB[:, g * 128:(g + 1) * 128], 32)
                        wait("sync", "pz", P.val("pz"))
                        dma("sync", yt_d[tcx], P5FB[:], sem="st")

        if build_upto >= 7:
            with contextlib.ExitStack() as ph:
                al = (lambda *a: ph.enter_context(sbt(*a))) if not flat else alloc
                YT = al("p6_YT", [128, 32 * 1024], BF16)
                ZSGU = al("p6_ZSGU", [32, 2 * NC * 1408], BF16)
                YGSB = al("p6_YGSB", [32, 1024], BF16)
                GWB = Buf(al("p6_GWB", [128, 2 * 4096], BF16), 4096)
                UWB = Buf(al("p6_UWB", [128, 2 * 4096], BF16), 4096)
                HCH = al("p6_HCH", [128, 1024], BF16)
                HSB = al("p6_HSB", [128, 1024], BF16)

                wait("sync", "st", P.val("st"))
                for tc in range(NTC):
                    for g in range(32):
                        dma("sync", YT[:, g * 1024 + tc * 128: g * 1024 + (tc + 1) * 128],
                            yt_d[tc][:, g * 128:(g + 1) * 128])
                for r in range(NC):
                    dma("sync", ZSGU[:, r * 1408:(r + 1) * 1408],
                        agz_out[r:r + 1, 0:ZSZ].rearrange("o (p f) -> (o p) f", p=32))
                    dma("sync", ZSGU[:, (NC + r) * 1408:(NC + r + 1) * 1408],
                        agz_out[r:r + 1, ZSZ:2 * ZSZ].rearrange("o (p f) -> (o p) f", p=32))
                v_zs = P.val("ld")
                wait("sync", "cc", cc_f)
                wait("tensor", "ld", v_zs)
                tyg = None
                for g in range(32):
                    for th in range(2):
                        tyg = op("tensor", lambda e, g=g, th=th: e.matmul(
                            PS[0:32, 512 * (4 + th):512 * (4 + th) + 512],
                            ZC32[:, 32 - g:64 - g], YT[:, g * 1024 + th * 512: g * 1024 + (th + 1) * 512],
                            start=(g == 0), stop=(g == 31)),
                            sem=("mm" if (g == 31 and th == 1) else None))
                wait("vector", "mm", tyg)
                op("vector", lambda e: e.tensor_copy(YGSB[:, 0:512], PS[0:32, 2048:2560]))
                vyg = op("vector", lambda e: e.tensor_copy(YGSB[:, 512:1024], PS[0:32, 2560:3072]), sem="ve")
                wait("tensor", "ve", vyg)
                rc = real_counts()
                glist = [(r, lc) for r in range(NC) for lc in range(rc[r])]
                gld = {}
                gu_mark = {}

                def load_gu(j):
                    r, lc = glist[j]
                    bufj = j % 2
                    if j >= 2:
                        wait("scalar", "mm", gu_mark[j - 2])
                    dma("scalar", GWB[bufj], agg_out[r:r + 1, lc * 128 * H: (lc + 1) * 128 * H].rearrange("o (p f) -> (o p) f", p=128))
                    gld[j] = dma("scalar", UWB[bufj], agu_out[r:r + 1, lc * 128 * H: (lc + 1) * 128 * H].rearrange("o (p f) -> (o p) f", p=128))
                load_gu(0)
                for j, (r, lc) in enumerate(glist):
                    bufj = j % 2
                    gi_real = sum(rc[:r]) + lc
                    wait("vector", "st", P.val("st"))
                    wait("tensor", "ld2", gld[j])
                    if j >= 1:
                        wait("tensor", "ve", P.val("ve"))
                    tgu = None
                    for mi in range(2):
                        wb = GWB if mi == 0 else UWB
                        for th in range(2):
                            bnk = 2 * mi + th
                            for g in range(32):
                                op("tensor", lambda e, g=g, th=th, bnk=bnk, wb=wb, bufj=bufj: e.matmul(
                                    bank(bnk), wb[bufj][:, g * 128:(g + 1) * 128],
                                    YT[:, g * 1024 + th * 512: g * 1024 + (th + 1) * 512],
                                    start=(g == 0), stop=False))
                            zoff = (0 if mi == 0 else NC) * 1408 + r * 1408 + lc * 128
                            tgu = op("tensor", lambda e, zoff=zoff, th=th, bnk=bnk: e.matmul(
                                bank(bnk), ZSGU[0:32, zoff:zoff + 128],
                                YGSB[0:32, th * 512:(th + 1) * 512], start=False, stop=True),
                                sem=("mm" if (mi == 1 and th == 1) else None))
                    gu_mark[j] = tgu
                    if j + 1 < len(glist):
                        load_gu(j + 1)
                    wait("scalar", "mm", tgu)
                    for th in range(2):
                        sv = op("scalar", lambda e, th=th: e.activation(HSB[:, th * 512:(th + 1) * 512], bank(th), AF.Silu), sem="sc")
                        wait("vector", "sc", sv)
                        op("vector", lambda e, th=th: e.tensor_tensor(HCH[:, th * 512:(th + 1) * 512], HSB[:, th * 512:(th + 1) * 512], bank(2 + th), OP.mult))
                    vh = op("vector", lambda e: e.tensor_copy(COLS[:, 15:16], COLS[:, 15:16]), sem="ve")
                    wait("sync", "ve", vh)
                    dma("sync", ht_d[gi_real], HCH[:], sem="st")
                    if "ht" in tap_t:
                        dma("sync", tap_t["ht"][gi_real], HCH[:], sem="st")

            with contextlib.ExitStack() as ph:
                al = (lambda *a: ph.enter_context(sbt(*a))) if not flat else alloc
                FT = al("p6d_FT", [128, NTC * 4096], BF16)
                DWB = Buf(al("p6d_DWB", [128, 2 * NCHP * 512], BF16), NCHP * 512)
                HTB = Buf(al("p6d_HTB", [128, 2 * 512], BF16), 512)
                DOB = al("p6d_DOB", [128, 512], BF16)
                L2A = al("p6d_BIGA", [128, 4096], F32)
                L2B = al("p6d_BIGB", [128, 4096], F32)
                L2C = al("p6d_BIGC", [128, 4096], F32)

                wait("sync", "st", P.val("st"))
                rc = real_counts()
                nreal = sum(rc)
                dn_mark = {}
                for ow in range(8):
                    for th in range(2):
                        dwl = {}
                        if ow or th:
                            wait("scalar", "mm", P.val("mm"))
                            wait("sync", "mm", P.val("mm"))

                        def load_dw(r2, idx):
                            off = ow * (128 * NCHP * 512)
                            if idx >= 2:
                                wait("scalar", "mm", dn_mark[(ow, th, idx - 2)])
                            dwl[r2] = dma("scalar", DWB[idx % 2],
                                          agd_out[r2:r2 + 1, off:off + 128 * NCHP * 512].rearrange("o (p f) -> (o p) f", p=128))
                        load_dw(0, 0)
                        wait("tensor", "ve", P.val("ve"))
                        cnt = 0
                        tdn = None
                        for r in range(NC):
                            if r + 1 < NC:
                                load_dw(r + 1, r + 1)
                            wait("tensor", "ld2", dwl[r])
                            for lc in range(rc[r]):
                                gi_real = sum(rc[:r]) + lc
                                hbuf = cnt % 2
                                if cnt >= 2:
                                    wait("sync", "mm", dn_mark.get((ow, th, "h", cnt - 2), 0))
                                hv = dma("sync", HTB[hbuf], ht_d[gi_real][:, th * 512:(th + 1) * 512])
                                wait("tensor", "ld", hv)
                                for oc4 in range(4):
                                    tdn = op("tensor", lambda e, lc=lc, oc4=oc4, r=r, hbuf=hbuf, cnt=cnt:
                                             e.matmul(bank(oc4), DWB[r % 2][:, lc * 512 + oc4 * 128: lc * 512 + (oc4 + 1) * 128],
                                                      HTB[hbuf], start=(cnt == 0), stop=(cnt == nreal - 1)),
                                             sem=("mm" if oc4 == 3 else None))
                                dn_mark[(ow, th, "h", cnt)] = tdn
                                cnt += 1
                            dn_mark[(ow, th, r)] = tdn
                        wait("vector", "mm", tdn)
                        for oc4 in range(4):
                            oc = ow * 4 + oc4
                            vv = op("vector", lambda e, oc4=oc4: e.tensor_copy(DOB[:], bank(oc4)), sem="ve")
                            wait("tensor", "ve", vv)
                            transpose_strip(lambda g: DOB[:, 128 * g:128 * (g + 1)],
                                            lambda g, oc=oc, th=th: FT[:, (th * 4 + g) * 4096 + oc * 128: (th * 4 + g) * 4096 + oc * 128 + 128], 4)
                v_ft = P.val("pz")
                wait("vector", "pz", v_ft)
                outv = out_own.rearrange("(c p) d -> c p d", p=128)
                for tc in range(NTC):
                    wait("sync", "ve", P.val("ve"))
                    vl = dma("sync", L2A[:], yf_d[tc])
                    wait("vector", "ld", vl)
                    op("vector", lambda e, tc=tc: e.tensor_tensor(L2B[:], FT[:, tc * 4096:(tc + 1) * 4096], L2A[:], OP.add))
                    vz = op("vector", lambda e: e.reduce_sum(COLS[:, 0:1], L2B[:], axis=AX.X), sem="ve")
                    vtrip(vz)
                    wait("scalar", "ve", vz)
                    sv = op("scalar", lambda e: e.activation(L2C[:], L2B[:], AF.Square, accum_out=COLS[:, 1:2]), sem="sc")
                    wait("vector", "sc", sv)
                    op("vector", lambda e: e.tensor_scalar(COLS[:, 2:3], COLS[:, 0:1], 1.0 / H, None, OP.mult))
                    vv9 = op("vector", lambda e: e.tensor_scalar(COLS[:, 3:4], COLS[:, 1:2], 1.0 / H, None, OP.mult), sem="ve")
                    vtrip(vv9)
                    vv9 = op("vector", lambda e: e.tensor_scalar(COLS[:, 4:5], COLS[:, 2:3], COLS[:, 2:3], None, OP.mult), sem="ve")
                    vtrip(vv9)
                    vz2 = op("vector", lambda e: e.tensor_scalar(COLS[:, 6:7], COLS[:, 3:4], COLS[:, 4:5], EPS, OP.subtract, OP.add), sem="ve")
                    vtrip(vz2)
                    wait("scalar", "ve", vz2)
                    sv2 = op("scalar", lambda e: e.activation(COLS[:, 7:8], COLS[:, 6:7], AF.Sqrt), sem="sc")
                    wait("vector", "sc", sv2)
                    vtrip(op("vector", lambda e: e.reciprocal(COLS[:, 8:9], COLS[:, 7:8]), sem="ve"))
                    vo = op("vector", lambda e: e.tensor_scalar(L2C[:], L2B[:], COLS[:, 2:3], COLS[:, 8:9], OP.subtract, OP.mult), sem="ve")
                    wait("sync", "ve", vo)
                    dma("sync", outv[tc], L2C[:], sem="st")

        wait("sync", "st", P.val("st"))
        wait("sync", "ld", P.val("ld"))
        wait("sync", "ld2", P.val("ld2"))
        wait("sync", "ld3", P.val("ld3"))
        wait("sync", "cc", P.val("cc"))

        emit_all(block, P, SEM)
    return nc, tap_t


def emit_all(block, P, SEM):
    @block.sync
    def _(e):
        for k, fn in P.items:
            if k == "sync":
                fn(e, SEM)

    @block.scalar
    def _(e):
        for k, fn in P.items:
            if k == "scalar":
                fn(e, SEM)

    @block.vector
    def _(e):
        for k, fn in P.items:
            if k == "vector":
                fn(e, SEM)

    @block.tensor
    def _(e):
        for k, fn in P.items:
            if k == "tensor":
                fn(e, SEM)

    @block.gpsimd
    def _(e):
        for k, fn in P.items:
            if k == "gpsimd":
                fn(e, SEM)


def _mk_e64():
    e = np.zeros((64, 64 * 128), ml_dtypes.bfloat16)
    for g in range(64):
        e[g, g * 128:(g + 1) * 128] = 1.0
    return e


def _host_consts():
    t = np.arange(S, dtype=np.float32)
    inv = 1.0 / (10000.0 ** (np.arange(0, D, 2, dtype=np.float32) / D))
    fr = np.einsum("i,j->ij", t, inv)
    emb = np.concatenate([fr, fr], -1)
    mt = np.where(np.arange(128)[None, :] <= np.arange(128)[:, None], 0.0, NEG).astype(np.float32)
    zc = np.zeros((128, 96), ml_dtypes.bfloat16)
    zc[:, 32] = 1.0
    return dict(cos_t=np.cos(emb).astype(np.float32), sin_t=np.sin(emb).astype(np.float32),
                ident=np.eye(128, dtype=ml_dtypes.bfloat16), identf=np.eye(128, dtype=np.float32),
                ones_f=np.ones((1, 128), np.float32), ones_b=np.ones((1, 128), ml_dtypes.bfloat16),
                ones_a=np.ones((128, 128), ml_dtypes.bfloat16),
                e64=_mk_e64(),
                mask_tok=mt, mask_T=np.ascontiguousarray(mt.T), zc32=zc)


def host_inputs(inputs):
    xs = np.ascontiguousarray(np.asarray(inputs["L0_attn_hidden_states"]).reshape(T, H))
    consts = _host_consts()
    maps = []
    for c in range(NC):
        m = {"x_own": np.ascontiguousarray(xs[TO * c:TO * (c + 1)])}
        for k in "qkvo":
            m[f"w_{k}"] = np.ascontiguousarray(np.asarray(inputs[f"L0_attn_w{k}"])[512 * c:512 * (c + 1)])
        cks = ffn_chunks(c)
        npad = NCHP - len(cks)

        def colsel(a, w):
            idx = [w * g + j for g in cks for j in range(w)]
            out = np.asarray(a)[:, idx]
            if npad:
                out = np.pad(out, [(0, 0), (0, w * npad)])
            return np.ascontiguousarray(out)

        m["g_qw"] = colsel(inputs["L0_ffn_gate_qweight"], 16).astype("<i4").view(np.uint8).reshape(H, NCHP * 64).copy()
        m["g_sc"] = colsel(inputs["L0_ffn_gate_scales"], 128)
        m["g_qz"] = colsel(inputs["L0_ffn_gate_qzeros"], 16).astype("<i4").view(np.uint8).reshape(32, NCHP * 64).copy()
        m["u_qw"] = colsel(inputs["L0_ffn_up_qweight"], 16).astype("<i4").view(np.uint8).reshape(H, NCHP * 64).copy()
        m["u_sc"] = colsel(inputs["L0_ffn_up_scales"], 128)
        m["u_qz"] = colsel(inputs["L0_ffn_up_qzeros"], 16).astype("<i4").view(np.uint8).reshape(32, NCHP * 64).copy()
        ridx = [128 * g + j for g in cks for j in range(128)]
        dq = np.asarray(inputs["L0_ffn_down_qweight"])[ridx, :]
        dq = np.pad(dq, [(0, 128 * npad), (0, 0)])
        m["d_qw"] = np.ascontiguousarray(dq).astype("<i4").view(np.uint8).reshape(NCHP * 128, 2048).copy()
        m["d_sc"] = np.ascontiguousarray(np.pad(np.asarray(inputs["L0_ffn_down_scales"])[cks, :], [(0, npad), (0, 0)]))
        m["d_qz"] = np.ascontiguousarray(np.pad(np.asarray(inputs["L0_ffn_down_qzeros"])[cks, :], [(0, npad), (0, 0)])).astype("<i4").view(np.uint8).reshape(NCHP, 2048).copy()
        m.update(consts)
        maps.append(m)
    return maps


def kernel(**inputs):
    maps = host_inputs(inputs)
    prog, _ = build_program()
    res = run_bass_kernel_spmd(prog, maps, core_ids=list(range(NC)))
    out = np.concatenate([res.results[c]["out_own"] for c in range(NC)], axis=0)
    return out.reshape(B, S, H).astype(np.float32)


# version 05 (final)
# speedup vs baseline: 1.0476x; 1.0476x over previous
import contextlib
import os
import numpy as np
import ml_dtypes
import concourse.bass as bass
import concourse.mybir as mybir
from concourse.bass_utils import run_bass_kernel_spmd

F32 = mybir.dt.float32
BF16 = mybir.dt.bfloat16
U8 = mybir.dt.uint8
AF = mybir.ActivationFunctionType
OP = mybir.AluOpType
AX = mybir.AxisListType

NC = 8
B, S, H = 4, 2048, 4096
NH, D = 32, 128
T = B * S
TO = T // NC
NTC = TO // 128
GTC = T // 128
HPC = NH // NC
IF = 11008
NCH = IF // 128
NCHP = 11
MAGIC = float(1.5 * 2.0 ** 23)
LN127 = float(np.log(np.float32(127.0)))
INVSQ = float(1.0 / np.sqrt(np.float64(D)))
EPS = 1e-5
NEG = -1.0e30

GSZ = NCHP * 128 * H
ZSZ = 32 * NCHP * 128
DSZ = 8 * 128 * NCHP * 512
OFF_G = 0
OFF_U = GSZ
OFF_ZG = 2 * GSZ
OFF_ZU = 2 * GSZ + ZSZ
OFF_D = 2 * GSZ + 2 * ZSZ
FSZ = 2 * GSZ + 2 * ZSZ + DSZ

AWQPOS = [(0, 0), (2, 0), (0, 1), (2, 1), (1, 0), (3, 0), (1, 1), (3, 1)]


def ffn_chunks(c):
    return list(range(NCHP * c, min(NCHP * c + NCHP, NCH)))


def real_counts():
    return [len(ffn_chunks(c)) for c in range(NC)]


class Prog:
    def __init__(self):
        self.items = []
        self.cnt = {}

    def on(self, key, fn):
        self.items.append((key, fn))

    def inc(self, name, n):
        self.cnt[name] = self.cnt.get(name, 0) + n
        return self.cnt[name]

    def val(self, name):
        return self.cnt.get(name, 0)


class Buf:

    def __init__(self, t, w):
        self.t, self.w = t, w

    def __getitem__(self, i):
        return self.t[:, i * self.w:(i + 1) * self.w]


def build_program(build_upto=99, taps=(), no_cc=False, no_pid=False, flat=False):
    nc = bass.Bass("TRN2", target_bir_lowering=False)
    dt = nc.dram_tensor
    P = Prog()

    x_own = dt("x_own", [TO, H], F32, kind="ExternalInput").ap()
    w_in = {k: dt(f"w_{k}", [512, H], F32, kind="ExternalInput").ap() for k in "qkvo"}
    g_qw = dt("g_qw", [H, NCHP * 64], U8, kind="ExternalInput").ap()
    g_sc = dt("g_sc", [32, NCHP * 128], F32, kind="ExternalInput").ap()
    g_qz = dt("g_qz", [32, NCHP * 64], U8, kind="ExternalInput").ap()
    u_qw = dt("u_qw", [H, NCHP * 64], U8, kind="ExternalInput").ap()
    u_sc = dt("u_sc", [32, NCHP * 128], F32, kind="ExternalInput").ap()
    u_qz = dt("u_qz", [32, NCHP * 64], U8, kind="ExternalInput").ap()
    d_qw = dt("d_qw", [NCHP * 128, 2048], U8, kind="ExternalInput").ap()
    d_sc = dt("d_sc", [NCHP, H], F32, kind="ExternalInput").ap()
    d_qz = dt("d_qz", [NCHP, 2048], U8, kind="ExternalInput").ap()
    cos_in = dt("cos_t", [S, D], F32, kind="ExternalInput").ap()
    sin_in = dt("sin_t", [S, D], F32, kind="ExternalInput").ap()
    ident_in = dt("ident", [128, 128], BF16, kind="ExternalInput").ap()
    identf_in = dt("identf", [128, 128], F32, kind="ExternalInput").ap()
    ones_in = dt("ones_f", [1, 128], F32, kind="ExternalInput").ap()
    onesb_in = dt("ones_b", [1, 128], BF16, kind="ExternalInput").ap()
    onesa_in = dt("ones_a", [128, 128], BF16, kind="ExternalInput").ap()
    e64_in = dt("e64", [64, 64 * 128], BF16, kind="ExternalInput").ap()
    mtok_in = dt("mask_tok", [128, 128], F32, kind="ExternalInput").ap()
    mtt_in = dt("mask_T", [128, 128], F32, kind="ExternalInput").ap()
    zc32_in = dt("zc32", [128, 96], BF16, kind="ExternalInput").ap()
    out_own = dt("out_own", [TO, H], F32, kind="ExternalOutput").ap()

    agx_in = dt("agx_in", [NTC, 128, H], BF16, kind="Internal").ap()
    agx_out = dt("agx_out", [NC * NTC, 128, H], BF16, kind="Internal").ap()
    agsx_in = dt("agsx_in", [128, NTC], F32, kind="Internal").ap()
    agsx_out = dt("agsx_out", [NC * 128, NTC], F32, kind="Internal").ap()
    agwo_in = dt("agwo_in", [32, 128, 512], BF16, kind="Internal").ap()
    agwo_out = dt("agwo_out", [NC * 32, 128, 512], BF16, kind="Internal").ap()
    agswo_in = dt("agswo_in", [1, 512], F32, kind="Internal").ap()
    agswo_out = dt("agswo_out", [NC, 512], F32, kind="Internal").ap()
    agg_in = dt("agg_in", [1, GSZ], BF16, kind="Internal").ap()
    agg_out = dt("agg_out", [NC, GSZ], BF16, kind="Internal").ap()
    agu_in = dt("agu_in", [1, GSZ], BF16, kind="Internal").ap()
    agu_out = dt("agu_out", [NC, GSZ], BF16, kind="Internal").ap()
    agz_in = dt("agz_in", [1, 2 * ZSZ], BF16, kind="Internal").ap()
    agz_out = dt("agz_out", [NC, 2 * ZSZ], BF16, kind="Internal").ap()
    agd_in = dt("agd_in", [1, DSZ], BF16, kind="Internal").ap()
    agd_out = dt("agd_out", [NC, DSZ], BF16, kind="Internal").ap()
    aga_in = dt("aga_in", [NC * 512, TO], BF16, kind="Internal").ap()
    aga_out = dt("aga_out", [NC * NC * 512, TO], BF16, kind="Internal").ap()
    agal = dt("agal", [NC, 512, TO], BF16, kind="Internal").ap()
    ags_in = dt("ags_in", [NC * 128, 32], F32, kind="Internal").ap()
    ags_out = dt("ags_out", [NC * NC * 128, 32], F32, kind="Internal").ap()
    agsl = dt("agsl", [NC, 128, 32], F32, kind="Internal").ap()
    qt_d = dt("qt_d", [HPC, 128, T], BF16, kind="Internal").ap()
    kt_d = dt("kt_d", [HPC, 128, T], BF16, kind="Internal").ap()
    vt_d = dt("vt_d", [T, HPC * 128], BF16, kind="Internal").ap()
    yf_d = dt("yf_d", [NTC, 128, H], F32, kind="Internal").ap()
    yt_d = dt("yt_d", [NTC, 128, H], BF16, kind="Internal").ap()
    ait_d = dt("ait_d", [NTC, 128, H], BF16, kind="Internal").ap()
    ht_d = dt("ht_d", [NCH, 128, TO], BF16, kind="Internal").ap()

    tap_t = {}
    for name, shape, dty in [
        ("xint", [NTC, 128, H], BF16), ("qt", [HPC, 128, T], BF16),
        ("kt", [HPC, 128, T], BF16), ("vt", [T, HPC * 128], BF16),
        ("attn", [HPC * 128, T], BF16), ("ssum", [128, 256], F32),
        ("y", [NTC, 128, H], F32), ("ht", [NCH, 128, TO], BF16),
        ("agf", [1, FSZ], BF16), ("aint", [NTC, 128, H], BF16),
        ("fqraw", [NTC, 128, H], BF16), ("cols", [NTC, 128, 16], F32),
        ("o1", [NTC, 128, H], F32),
    ]:
        if name in taps:
            tap_t[name] = dt("tap_" + name, shape, dty, kind="ExternalOutput").ap()

    agxv = agx_out.rearrange("(r c) p f -> r c p f", r=NC)
    agsxv = agsx_out.rearrange("(r p) f -> r p f", r=NC)
    agwov = agwo_out.rearrange("(r g) p f -> r g p f", r=NC)
    agai = aga_in.rearrange("(d g) t -> d g t", d=NC)
    agav = aga_out.rearrange("(x g) t -> x g t", x=NC * NC)
    agsi = ags_in.rearrange("(d p) f -> d p f", d=NC)
    agsv = ags_out.rearrange("(x p) f -> x p f", x=NC * NC)

    est = contextlib.ExitStack()
    sbt = nc.sbuf_tensor

    def alloc(*args):
        return est.enter_context(sbt(*args))

    with est, \
         nc.psum_tensor("PS", [128, 3072], F32) as PS, \
         nc.psum_tensor("PSB", [128, 2048], BF16) as PSB, \
         nc.semaphore("s_ld") as s_ld, \
         nc.semaphore("s_ld2") as s_ld2, \
         nc.semaphore("s_ld3") as s_ld3, \
         nc.semaphore("s_mm") as s_mm, \
         nc.semaphore("s_ve") as s_ve, \
         nc.semaphore("s_sc") as s_sc, \
         nc.semaphore("s_cc") as s_cc, \
         nc.semaphore("s_st") as s_st, \
         nc.semaphore("s_pz") as s_pz, \
         nc.Block() as block:
        SEM = dict(ld=s_ld, ld2=s_ld2, ld3=s_ld3, mm=s_mm, ve=s_ve, sc=s_sc,
                   cc=s_cc, st=s_st, pz=s_pz)

        def bank(i):
            return PS[:, 512 * i:512 * (i + 1)]

        pid_cache = {}

        def get_pid(e):
            if "v" not in pid_cache:
                pid_cache["v"] = e.partition_id()
            return pid_cache["v"]

        if not no_pid:
            P.on("sync", lambda e, s: (get_pid(e), None)[1])

        DMA_SERIAL = bool(int(os.environ.get("KSERIAL", "0")))

        def dma(q, out, in_, sem=None):
            if sem is None:
                sem = {"sync": "ld", "scalar": "ld2", "gpsimd": "ld3"}[q]
            v = P.inc(sem, 16)
            P.on(q, lambda e, s, out=out, in_=in_, sem=sem:
                 e.dma_start(out=out, in_=in_).then_inc(s[sem], 16))
            if DMA_SERIAL:
                P.on(q, lambda e, s, sem=sem, v=v: e.wait_ge(s[sem], v))
            return v

        def dyn_dma(q, out_fn, sem="ld"):
            v = P.inc(sem, 16)
            def run(e, s, out_fn=out_fn, sem=sem):
                o, i = out_fn(e)
                e.dma_start(out=o, in_=i).then_inc(s[sem], 16)
            P.on(q, run)
            return v

        def wait(key, sem, v):
            if v:
                P.on(key, lambda e, s, sem=sem, v=v: e.wait_ge(s[sem], v))

        def op(key, fn, sem=None, n=1):
            if sem is None:
                P.on(key, lambda e, s, fn=fn: fn(e))
                return None
            v = P.inc(sem, n)
            P.on(key, lambda e, s, fn=fn, sem=sem, n=n: fn(e).then_inc(s[sem], n))
            return v

        def vtrip(v=None):
            if v is None:
                v = P.val("ve")
            wait("vector", "ve", v)

        def ag(in_ap, out_ap):
            if no_cc:
                return 0
            v = P.inc("cc", 1)
            P.on("gpsimd", lambda e, s, in_ap=in_ap, out_ap=out_ap: e.collective_compute(
                "AllGather", OP.bypass, ins=[in_ap], outs=[out_ap],
                replica_groups=[list(range(NC))]).then_inc(s["cc"], 1))
            return v

        IDENT = alloc("IDENT", [128, 128], BF16)
        IDENTF = alloc("IDENTF", [128, 128], F32)
        ONESF = alloc("ONESF", [1, 128], F32)
        ONESB = alloc("ONESB", [1, 128], BF16)
        ONESA = alloc("ONESA", [128, 128], BF16)
        MASKT = alloc("MASKT", [128, 128], F32)
        MASKTT = alloc("MASKTT", [128, 128], F32)
        ZC32 = alloc("ZC32", [128, 96], BF16)
        COLS = alloc("COLS", [128, 16], F32)
        SXO = alloc("SXO", [128, NTC], F32)
        SACOL = alloc("SACOL", [128, NTC], F32)

        def transpose_strip(blk_in, blk_out, n):
            wait("tensor", "pz", P.val("pz"))
            wait("vector", "st", P.val("st"))
            ev = []
            nb = (n + 7) // 8
            for j in range(nb):
                boff = (j % 2) * 1024
                if j >= 2:
                    wait("tensor", "pz", ev[j - 2])
                g0, g1 = 8 * j, min(8 * j + 8, n)
                tv = None
                for g in range(g0, g1):
                    src_ap = blk_in(g)
                    slot = boff + 128 * (g - g0)
                    tv = op("tensor", lambda e, src_ap=src_ap, slot=slot: e.transpose(
                        PSB[:, slot:slot + 128], src_ap, IDENT[:]),
                        sem=("mm" if g == g1 - 1 else None))
                wait("vector", "mm", tv)
                pv = None
                for g in range(g0, g1):
                    dst_ap = blk_out(g)
                    slot = boff + 128 * (g - g0)
                    pv = op("vector", lambda e, dst_ap=dst_ap, slot=slot: e.tensor_copy(
                        dst_ap, PSB[:, slot:slot + 128]),
                        sem=("pz" if g == g1 - 1 else None))
                ev.append(pv)
            return ev[-1]

        for dst, src in [(IDENT[:], ident_in), (IDENTF[:], identf_in),
                         (ONESF[:], ones_in), (ONESB[:], onesb_in), (ONESA[:], onesa_in),
                         (MASKT[:], mtok_in), (MASKTT[:], mtt_in), (ZC32[:], zc32_in)]:
            dma("sync", dst, src)
        v0 = P.val("ld")
        wait("sync", "ld", v0)
        for k in ("vector", "tensor", "scalar"):
            wait(k, "ld", v0)

        est_w = contextlib.ExitStack()
        _alw = (lambda *a: est_w.enter_context(sbt(*a))) if not flat else alloc
        WQKVT = _alw("WQKVT", [128, 32 * 1536], BF16)
        SWB = _alw("SWB", [128, 1536], F32)

        with contextlib.ExitStack() as ph:
            al = (lambda *a: ph.enter_context(sbt(*a))) if not flat else alloc
            BIGA = al("p1_BIGA", [128, 4096], F32)
            BIGB = al("p1_BIGB", [128, 4096], F32)
            BFA = al("p1_BFA", [128, 4096], BF16)
            BFB = al("p1_BFB", [128, 4096], BF16)
            SWROW = al("p1_SWROW", [1, 1536], F32)
            SWOR = al("p1_SWOR", [1, 512], F32)

            xv = x_own.rearrange("(c p) d -> c p d", p=128)
            for tc in range(NTC):
                if tc > 0:
                    wait("sync", "ve", P.val("ve"))
                vl = dma("sync", BIGA[:], xv[tc])
                wait("vector", "ld", vl)
                vtrip(op("vector", lambda e: e.reduce_max(COLS[:, 0:1], BIGA[:], axis=AX.X, apply_absolute_value=True), sem="ve"))
                vtrip(op("vector", lambda e, tc=tc: e.tensor_scalar(SXO[:, tc:tc + 1], COLS[:, 0:1], 1.0 / 127.0, 1e-8, OP.mult, OP.max), sem="ve"))
                vtrip(op("vector", lambda e, tc=tc: e.reciprocal(COLS[:, 1:2], SXO[:, tc:tc + 1]), sem="ve"))
                op("vector", lambda e: e.tensor_scalar(BIGB[:], BIGA[:], COLS[:, 1:2], MAGIC, OP.mult, OP.add))
                vi = op("vector", lambda e: e.tensor_scalar(BFA[:], BIGB[:], -MAGIC, None, OP.add), sem="ve")
                if "fqraw" in tap_t:
                    wait("sync", "ve", vi)
                    dma("sync", tap_t["fqraw"][tc], BFA[:], sem="st")
                    dma("sync", tap_t["cols"][tc], COLS[:], sem="st")
                wait("tensor", "ve", vi)
                transpose_strip(lambda g: BFA[:, 128 * g:128 * (g + 1)],
                                lambda g: BFB[:, 128 * g:128 * (g + 1)], 32)
                wait("sync", "pz", P.val("pz"))
                dma("sync", agx_in[tc], BFB[:], sem="st")
                if "xint" in tap_t:
                    dma("sync", tap_t["xint"][tc], BFB[:], sem="st")
            wait("sync", "ve", P.val("ve"))
            dma("sync", agsx_in[:], SXO[:], sem="st")
            wait("gpsimd", "st", P.val("st"))
            cc_x = ag(agx_in[:], agx_out[:])
            cc_sx = ag(agsx_in[:], agsx_out[:])

            for wi, k in (enumerate("qkvo") if build_upto >= 2 else []):
                wv = w_in[k].rearrange("(c p) d -> c p d", p=128)
                for oc in range(4):
                    wait("sync", "ve", P.val("ve"))
                    vl = dma("sync", BIGA[:], wv[oc])
                    wait("vector", "ld", vl)
                    vtrip(op("vector", lambda e: e.reduce_max(COLS[:, 0:1], BIGA[:], axis=AX.X, apply_absolute_value=True), sem="ve"))
                    vtrip(op("vector", lambda e: e.tensor_scalar(COLS[:, 2:3], COLS[:, 0:1], 1.0 / 7.0, 1e-8, OP.mult, OP.max), sem="ve"))
                    vtrip(op("vector", lambda e: e.reciprocal(COLS[:, 1:2], COLS[:, 2:3]), sem="ve"))
                    op("vector", lambda e: e.tensor_scalar(BIGB[:], BIGA[:], COLS[:, 1:2], MAGIC, OP.mult, OP.add))
                    vi = op("vector", lambda e: e.tensor_scalar(BFA[:], BIGB[:], -MAGIC, None, OP.add), sem="ve")
                    wait("tensor", "ve", vi)
                    tv = op("tensor", lambda e: e.transpose(PS[0:1, 0:128], COLS[:, 2:3], IDENTF[:]), sem="mm")
                    wait("vector", "mm", tv)
                    rdst = (SWROW[:, wi * 512 + oc * 128: wi * 512 + (oc + 1) * 128]
                            if k != "o" else SWOR[:, oc * 128:(oc + 1) * 128])
                    vz = op("vector", lambda e, rdst=rdst: e.tensor_copy(rdst, PS[0:1, 0:128]), sem="pz")
                    wait("tensor", "pz", vz)
                    if k != "o":
                        transpose_strip(lambda g: BFA[:, 128 * g:128 * (g + 1)],
                                        lambda g, wi=wi, oc=oc: WQKVT[:, g * 1536 + wi * 512 + oc * 128: g * 1536 + wi * 512 + (oc + 1) * 128], 32)
                    else:
                        transpose_strip(lambda g: BFA[:, 128 * g:128 * (g + 1)],
                                        lambda g: BFB[:, 128 * g:128 * (g + 1)], 32)
                        wait("sync", "pz", P.val("pz"))
                        for g in range(32):
                            dma("sync", agwo_in[g][:, oc * 128:(oc + 1) * 128], BFB[:, 128 * g:128 * (g + 1)], sem="st")
            cc_wo = cc_swo = 0
            if build_upto >= 2:
                wait("tensor", "pz", P.val("pz"))
                for j in range(3):
                    tv = op("tensor", lambda e, j=j: e.matmul(bank(0), ONESF[:], SWROW[:, j * 512:(j + 1) * 512], start=True, stop=True), sem="mm")
                    wait("vector", "mm", tv)
                    vz = op("vector", lambda e, j=j: e.tensor_copy(SWB[:, j * 512:(j + 1) * 512], bank(0)), sem="pz")
                    wait("tensor", "pz", vz)
                wait("sync", "pz", P.val("pz"))
                dma("sync", agswo_in[:], SWOR[:], sem="st")
                wait("gpsimd", "st", P.val("st"))
                cc_wo = ag(agwo_in[:], agwo_out[:])
                cc_swo = ag(agswo_in[:], agswo_out[:])

        if build_upto >= 3:

            if build_upto >= 4:
                with contextlib.ExitStack() as ph:
                    al = (lambda *a: ph.enter_context(sbt(*a))) if not flat else alloc
                    COSB = al("p3_COSB", [128, 16 * 128], F32)
                    SINB = al("p3_SINB", [128, 16 * 128], F32)
                    SXALL = al("p3_SXALL", [128, GTC], F32)
                    XTT = Buf(al("p3_XTT", [128, 2 * 4096], BF16), 4096)
                    QKVE = al("p3_QKVE", [128, 1536], F32)
                    ROPC = al("p3_ROPC", [128, 512], F32)
                    ROPS = al("p3_ROPS", [128, 512], F32)
                    QKI = al("p3_QKI", [128, 1024], BF16)
                    INVC = al("p3_INVC", [128, 16], F32)
                    QTS = al("p3_QTS", [128, 512], BF16)
                    KTS = al("p3_KTS", [128, 512], BF16)
                    ESC = al("p3_ESC", [128, 512], BF16)
                    QKB = al("p3_BIGB", [128, 1536], F32)
                    QKC = al("p3_BIGC", [128, 2048], F32)
                    VFB = al("p3_BFB", [128, 512], BF16)

                    cosv = cos_in.rearrange("(c p) d -> c p d", p=128)
                    sinv = sin_in.rearrange("(c p) d -> c p d", p=128)
                    for scn in range(16):
                        dma("sync", COSB[:, scn * 128:(scn + 1) * 128], cosv[scn])
                        dma("sync", SINB[:, scn * 128:(scn + 1) * 128], sinv[scn])
                    wait("sync", "cc", cc_sx)
                    for r in range(NC):
                        dma("sync", SXALL[:, r * NTC:(r + 1) * NTC], agsxv[r])
                    wait("vector", "ld", P.val("ld"))
                    wait("sync", "cc", cc_x)
                    mm_mark = {}
                    for tcg in range(GTC):
                        buf = tcg % 2
                        bk = (tcg % 2) * 3
                        wait("vector", "st", P.val("st"))
                        if tcg >= 2:
                            wait("sync", "mm", mm_mark[tcg - 2])
                        vx = dma("sync", XTT[buf], agxv[tcg // NTC][tcg % NTC])
                        if tcg >= 2:
                            wait("tensor", "ve", P.val("ve"))
                        wait("tensor", "ld", vx)
                        tmv = None
                        for g in range(32):
                            for j3 in range(3):
                                sp = (g == 31 and j3 == 2)
                                tmv2 = op("tensor", lambda e, g=g, j3=j3, bk=bk, buf=buf:
                                          e.matmul(bank(bk + j3), XTT[buf][:, 128 * g:128 * (g + 1)],
                                                   WQKVT[:, g * 1536 + j3 * 512: g * 1536 + (j3 + 1) * 512],
                                                   start=(g == 0), stop=(g == 31)),
                                          sem=("mm" if sp else None))
                                if sp:
                                    tmv = tmv2
                        mm_mark[tcg] = tmv
                        wait("vector", "mm", tmv)
                        scn = tcg % 16
                        for j3 in range(3):
                            op("vector", lambda e, j3=j3, bk=bk, tcg=tcg: e.tensor_scalar(
                                QKB[:, j3 * 512:(j3 + 1) * 512], bank(bk + j3),
                                SXALL[:, tcg:tcg + 1], None, OP.mult))
                        op("vector", lambda e: e.tensor_tensor(QKVE[:], QKB[:, 0:1536], SWB[:], OP.mult), sem="ve")
                        for base in (0, 512):
                            for hl in range(HPC):
                                c0, h0, cs = base + hl * 128, hl * 128, scn * 128
                                op("vector", lambda e, c0=c0, h0=h0, cs=cs: e.tensor_tensor(
                                    ROPC[:, h0:h0 + 128], QKVE[:, c0:c0 + 128], COSB[:, cs:cs + 128], OP.mult))
                                op("vector", lambda e, c0=c0, h0=h0, cs=cs: e.tensor_tensor(
                                    ROPS[:, h0:h0 + 64], QKVE[:, c0 + 64:c0 + 128], SINB[:, cs:cs + 64], OP.mult))
                                op("vector", lambda e, c0=c0, h0=h0, cs=cs: e.tensor_tensor(
                                    ROPS[:, h0 + 64:h0 + 128], QKVE[:, c0:c0 + 64], SINB[:, cs + 64:cs + 128], OP.mult))
                            for hl in range(HPC):
                                h0 = hl * 128
                                op("vector", lambda e, h0=h0, base=base: e.tensor_tensor(
                                    QKC[:, base + h0:base + h0 + 64], ROPC[:, h0:h0 + 64], ROPS[:, h0:h0 + 64], OP.subtract))
                                op("vector", lambda e, h0=h0, base=base: e.tensor_tensor(
                                    QKC[:, base + h0 + 64:base + h0 + 128], ROPC[:, h0 + 64:h0 + 128], ROPS[:, h0 + 64:h0 + 128], OP.add))
                        srcs12 = ([("qk", hl * 128) for hl in range(HPC)] +
                                  [("qk", 512 + hl * 128) for hl in range(HPC)] +
                                  [("v", 1024 + hl * 128) for hl in range(HPC)])
                        rv = None
                        for ri, (kind, c0) in enumerate(srcs12):
                            srcb = QKC if kind == "qk" else QKVE
                            rv = op("vector", lambda e, ri=ri, c0=c0, srcb=srcb: e.reduce_max(
                                COLS[:, ri:ri + 1], srcb[:, c0:c0 + 128], axis=AX.X, apply_absolute_value=True),
                                sem=("ve" if ri == 11 else None))
                        vtrip(rv)
                        for ri in range(4):
                            op("vector", lambda e, ri=ri: e.tensor_scalar(
                                INVC[:, 12 + ri:13 + ri], COLS[:, ri:ri + 1], INVSQ / 127.0, 1e-8 * INVSQ, OP.mult, OP.max))
                        vsc12 = None
                        for ri in range(12):
                            vsc12 = op("vector", lambda e, ri=ri: e.tensor_scalar(COLS[:, ri:ri + 1], COLS[:, ri:ri + 1], 1.0 / 127.0, 1e-8, OP.mult, OP.max),
                                       sem=("ve" if ri == 11 else None))
                        vtrip(vsc12)
                        rvv = None
                        for ri in range(12):
                            rvv = op("vector", lambda e, ri=ri: e.reciprocal(INVC[:, ri:ri + 1], COLS[:, ri:ri + 1]),
                                     sem=("ve" if ri == 11 else None))
                        vtrip(rvv)
                        for ri, (kind, c0) in enumerate(srcs12):
                            srcb = QKC if kind == "qk" else QKVE
                            op("vector", lambda e, ri=ri, c0=c0, srcb=srcb: e.tensor_scalar(
                                QKC[:, 1536:1664], srcb[:, c0:c0 + 128], INVC[:, ri:ri + 1], MAGIC, OP.mult, OP.add))
                            dsti = QKI[:, c0:c0 + 128] if c0 < 1024 else QKC[:, 1664:1792]
                            op("vector", lambda e, dsti=dsti: e.tensor_scalar(dsti, QKC[:, 1536:1664], -MAGIC, None, OP.add))
                            if ri < 8:
                                dst = QTS if ri < 4 else KTS
                                scl = INVC[:, 12 + ri:13 + ri] if ri < 4 else COLS[:, ri:ri + 1]
                                hl = ri % 4
                                op("vector", lambda e, c0=c0, dst=dst, hl=hl, scl=scl: e.tensor_scalar(
                                    dst[:, hl * 128:(hl + 1) * 128], QKI[:, c0:c0 + 128], scl, None, OP.mult))
                            else:
                                hl = ri % 4
                                op("vector", lambda e, hl=hl, ri=ri: e.tensor_scalar(
                                    VFB[:, hl * 128:(hl + 1) * 128], QKC[:, 1664:1792], COLS[:, ri:ri + 1], None, OP.mult))
                        vqk = op("vector", lambda e: e.tensor_copy(COLS[:, 15:16], COLS[:, 15:16]), sem="ve")
                        wait("sync", "ve", vqk)
                        dma("sync", vt_d[tcg * 128:(tcg + 1) * 128, :], VFB[:], sem="st")
                        if "vt" in tap_t:
                            dma("sync", tap_t["vt"][tcg * 128:(tcg + 1) * 128, :], VFB[:], sem="st")
                        wait("tensor", "ve", vqk)
                        for srcb, dstd, tapn in ((QTS, qt_d, "qt"), (KTS, kt_d, "kt")):
                            transpose_strip(lambda g, srcb=srcb: srcb[:, 128 * g:128 * (g + 1)],
                                            lambda g: ESC[:, g * 128:(g + 1) * 128], HPC)
                            wait("sync", "pz", P.val("pz"))
                            dsl = dstd[:, :, tcg * 128:(tcg + 1) * 128].rearrange("h p t -> p h t")
                            sslv = ESC[:].rearrange("p (h t) -> p h t", h=HPC)
                            dma("sync", dsl, sslv, sem="st")
                            if tapn in tap_t:
                                dma("sync", tap_t[tapn][:, :, tcg * 128:(tcg + 1) * 128].rearrange("h p t -> p h t"), sslv, sem="st")

            with contextlib.ExitStack() as ph:
                al = (lambda *a: ph.enter_context(sbt(*a))) if not flat else alloc
                E64 = al("p2b_E64", [64, 64 * 128], BF16)
                DQB = al("p2b_BIGB", [128, 2048], F32)
                DQC = al("p2b_BIGC", [128, 512], F32)
                QBY = al("p2b_QBY", [128, 2048], U8)
                QHI = al("p2b_QHI", [128, 2048], U8)
                WDQ = al("p2b_WDQ", [128, NCHP * 128], BF16)
                SGROWS = al("p2b_SGROWS", [64, NCHP * 128], BF16)
                SGF = al("p2b_SGF", [32, NCHP * 128], F32)
                ZROW = al("p2b_ZROW", [32, NCHP * 64], F32)
                DZN = al("p2b_DZN", [NCHP, 2048], U8)
                DZF = al("p2b_DZF", [NCHP, H], BF16)
                DSB = al("p2b_DSB", [NCHP, H], BF16)
                DZB = al("p2b_DZB", [NCHP, H], BF16)
                wait("sync", "st", P.val("st"))
                wait("gpsimd", "st", P.val("st"))
                wait("gpsimd", "pz", P.val("pz"))
                wait("gpsimd", "ve", P.val("ve"))
                wait("gpsimd", "mm", P.val("mm"))
                dma("sync", E64[:], e64_in)
                wait("vector", "ld", P.val("ld"))
                wait("tensor", "ld", P.val("ld"))

                dma("gpsimd", SGROWS[0:32, :], g_sc)
                dma("gpsimd", SGROWS[32:64, :], u_sc)
                v_srow = P.val("ld3")

                for m, (qz, scs) in enumerate([(g_qz, g_sc), (u_qz, u_sc)]):
                    wait("sync", "ve", P.val("ve"))
                    wait("vector", "st", P.val("st"))
                    vl = dma("sync", QBY[0:32, 0:704], qz)
                    vl2 = dma("sync", SGF[:], scs)
                    wait("vector", "ld", max(vl, vl2))
                    op("vector", lambda e: e.tensor_scalar(QHI[0:32, 0:704], QBY[0:32, 0:704], 4, None, OP.logical_shift_right))
                    op("vector", lambda e: e.scalar_tensor_tensor(ZROW[:], QHI[0:32, 0:704], -16.0, QBY[0:32, 0:704], OP.mult, OP.add))
                    last = None
                    for pos in range(8):
                        byi, half = AWQPOS[pos]
                        def fn(e, byi=byi, half=half, pos=pos):
                            srcap = ZROW[:] if half == 0 else QHI[0:32, 0:704]
                            srcs = srcap.rearrange("p (c b) -> p c b", b=4)[:, :, byi]
                            dsts = WDQ[0:32, :].rearrange("p (c j) -> p c j", j=8)[:, :, pos]
                            ssrc = SGF[:].rearrange("p (c j) -> p c j", j=8)[:, :, pos]
                            return e.scalar_tensor_tensor(dsts, srcs, -1.0, ssrc, OP.mult, OP.mult)
                        last = op("vector", fn, sem="ve")
                    wait("sync", "ve", last)
                    off = 0 if m == 0 else ZSZ
                    dma("sync", agz_in[:, off:off + ZSZ].rearrange("o (p f) -> (o p) f", p=32), WDQ[0:32, :], sem="st")

                wait("tensor", "ld3", v_srow)
                for m, qw in enumerate([g_qw, u_qw]):
                    qwv = qw.rearrange("(g p) c -> g p c", p=128)
                    for g in range(32):
                        wait("sync", "ve", P.val("ve"))
                        wait("vector", "st", P.val("st"))
                        vl = dma("sync", QBY[:, 0:704], qwv[g])
                        row = m * 32 + g
                        wait("tensor", "ve", P.val("ve"))
                        tvl = None
                        for wnd, wid in enumerate([512, 512, 384]):
                            tvl = op("tensor", lambda e, wnd=wnd, wid=wid, row=row: e.matmul(
                                PS[:, 512 * wnd:512 * wnd + wid], E64[:, row * 128:(row + 1) * 128],
                                SGROWS[:, 512 * wnd:512 * wnd + wid],
                                start=True, stop=True), sem="mm")
                        wait("vector", "mm", tvl)
                        wait("vector", "ld", vl)
                        op("vector", lambda e: e.tensor_scalar(QHI[:, 0:704], QBY[:, 0:704], 4, None, OP.logical_shift_right))
                        op("vector", lambda e: e.scalar_tensor_tensor(DQB[:, 0:704], QHI[:, 0:704], -16.0, QBY[:, 0:704], OP.mult, OP.add))
                        last = None
                        for pos in range(8):
                            byi, half = AWQPOS[pos]
                            def fn(e, byi=byi, half=half, pos=pos):
                                srcap = DQB[:, 0:704] if half == 0 else QHI[:, 0:704]
                                srcs = srcap.rearrange("p (c b) -> p c b", b=4)[:, :, byi]
                                dsts = WDQ[:].rearrange("p (c j) -> p c j", j=8)[:, :, pos]
                                sbs = PS[:, 0:1408].rearrange("p (c j) -> p c j", j=8)[:, :, pos]
                                return e.tensor_tensor(dsts, srcs, sbs, OP.mult)
                            last = op("vector", fn, sem="ve")
                        wait("sync", "ve", last)
                        agio = agg_in if m == 0 else agu_in
                        for lc in range(NCHP):
                            dst = agio[:, lc * (128 * H): (lc + 1) * (128 * H)]
                            dstv = dst.rearrange("o (p f) -> (o p) f", p=128)
                            dma("sync", dstv[:, g * 128:(g + 1) * 128], WDQ[:, lc * 128:(lc + 1) * 128], sem="st")

                wait("sync", "ve", P.val("ve"))
                vl = dma("sync", DZN[:], d_qz)
                vl2 = dma("gpsimd", DSB[:], d_sc)
                wait("vector", "ld", vl)
                wait("vector", "ld3", vl2)
                op("vector", lambda e: e.tensor_scalar(QHI[0:NCHP, 0:2048], DZN[:], 4, None, OP.logical_shift_right))
                op("vector", lambda e: e.scalar_tensor_tensor(DQB[0:NCHP, 0:2048], QHI[0:NCHP, 0:2048], -16.0, DZN[:], OP.mult, OP.add))
                for pos in range(8):
                    byi, half = AWQPOS[pos]
                    def fn(e, byi=byi, half=half, pos=pos):
                        srcap = DQB[0:NCHP, 0:2048] if half == 0 else QHI[0:NCHP, 0:2048]
                        srcs = srcap.rearrange("p (c b) -> p c b", b=4)[:, :, byi]
                        dsts = DZF[:].rearrange("p (c j) -> p c j", j=8)[:, :, pos]
                        return e.tensor_copy(dsts, srcs)
                    op("vector", fn)
                vzf = op("vector", lambda e: e.tensor_tensor(DZB[:], DZF[:], DSB[:], OP.mult), sem="ve")
                dqwv = d_qw.rearrange("(l p) c -> l p c", p=128)
                wait("tensor", "ve", vzf)
                for lc in range(NCHP):
                    wait("sync", "ve", P.val("ve"))
                    vl = dma("sync", QBY[:, 0:2048], dqwv[lc])
                    wait("vector", "ld", vl)
                    op("vector", lambda e: e.tensor_scalar(QHI[:, 0:2048], QBY[:, 0:2048], 4, None, OP.logical_shift_right))
                    op("vector", lambda e: e.scalar_tensor_tensor(DQB[:, 0:2048], QHI[:, 0:2048], -16.0, QBY[:, 0:2048], OP.mult, OP.add))
                    for w8 in range(8):
                        wait("tensor", "ve", P.val("ve"))
                        wait("vector", "st", P.val("st"))
                        op("tensor", lambda e, w8=w8, lc=lc: e.matmul(bank(0), E64[0:NCHP, lc * 128:(lc + 1) * 128], DSB[:, w8 * 512:(w8 + 1) * 512], start=True, stop=True))
                        tv2 = op("tensor", lambda e, w8=w8, lc=lc: e.matmul(bank(1), E64[0:NCHP, lc * 128:(lc + 1) * 128], DZB[:, w8 * 512:(w8 + 1) * 512], start=True, stop=True), sem="mm")
                        wait("vector", "mm", tv2)
                        for pos in range(8):
                            byi, half = AWQPOS[pos]
                            def fn(e, byi=byi, half=half, pos=pos, w8=w8):
                                srcap = DQB[:, 0:2048] if half == 0 else QHI[:, 0:2048]
                                srcs = srcap.rearrange("p (c b) -> p c b", b=4)[:, w8 * 64:(w8 + 1) * 64, byi]
                                sbs = bank(0).rearrange("p (c j) -> p c j", j=8)[:, :, pos]
                                tmp = DQC[:, 0:512].rearrange("p (c j) -> p c j", j=8)[:, :, pos]
                                return e.tensor_tensor(tmp, srcs, sbs, OP.mult)
                            op("vector", fn)
                        op("vector", lambda e: e.tensor_tensor(DQC[:, 0:512], DQC[:, 0:512], bank(1), OP.subtract))
                        vv = op("vector", lambda e: e.tensor_copy(WDQ[:, 0:512], DQC[:, 0:512]), sem="ve")
                        wait("sync", "ve", vv)
                        off = w8 * (128 * NCHP * 512)
                        dstv = agd_in[:, off:off + 128 * NCHP * 512].rearrange("o (p f) -> (o p) f", p=128)
                        dma("sync", dstv[:, lc * 512:(lc + 1) * 512], WDQ[:, 0:512], sem="st")
                wait("gpsimd", "st", P.val("st"))
                cc_g = ag(agg_in[:], agg_out[:])
                cc_u = ag(agu_in[:], agu_out[:])
                cc_z = ag(agz_in[:], agz_out[:])
                cc_f = ag(agd_in[:], agd_out[:])
                if "agf" in tap_t:
                    dma("sync", tap_t["agf"][:, 0:GSZ], agg_in[:], sem="st")
                    dma("sync", tap_t["agf"][:, GSZ:2 * GSZ], agu_in[:], sem="st")
                    dma("sync", tap_t["agf"][:, OFF_ZG:OFF_ZG + 2 * ZSZ], agz_in[:], sem="st")
                    dma("sync", tap_t["agf"][:, OFF_D:OFF_D + DSZ], agd_in[:], sem="st")
                if build_upto < 3:
                    pass
        if not flat:
            est_w.close()

        if build_upto >= 5:
            with contextlib.ExitStack() as ph:
                al = (lambda *a: ph.enter_context(sbt(*a))) if not flat else alloc
                QTB = Buf(al("p4_QTB", [128, 2 * S], BF16), S)
                KTB = Buf(al("p4_KTB", [128, 2 * S], BF16), S)
                VB = Buf(al("p4_VB", [128, 2 * 2048], BF16), 2048)
                ESCT = al("p4_ESCT", [128, 2048], BF16)
                E1 = Buf(al("p4_E1", [128, 3 * 512], F32), 512)
                ESC3 = Buf(al("p4_ESC3", [128, 3 * 512], BF16), 512)
                PINT = Buf(al("p4_PINT", [128, 3 * 512], BF16), 512)
                MCOL = al("p4_MCOL", [128, 16], F32)
                SCOL = al("p4_SCOL", [128, 16], F32)
                MNEG = al("p4_MNEG", [128, 1], F32)
                MRSB = al("p4_MRSB", [1, 2048], F32)
                MB = al("p4_MB", [128, 2048], F32)
                ATE = al("p4_ATE", [128, 2048], BF16)
                STILE = al("p4_STILE", [128, 256], F32)

                wait("sync", "st", P.val("st"))
                bh_list = [(b, hl) for b in range(4) for hl in range(HPC)]
                ld_bh = {}

                def load_bh(i):
                    b, hl = bh_list[i]
                    bf = i % 2
                    v1 = dma("sync", QTB[bf], qt_d[hl][:, b * S:(b + 1) * S])
                    v2 = dma("sync", KTB[bf], kt_d[hl][:, b * S:(b + 1) * S])
                    v3 = 0
                    for tkc in range(16):
                        v3 = dma("sync", VB[bf][:, tkc * 128:(tkc + 1) * 128],
                                 vt_d[b * S + tkc * 128: b * S + (tkc + 1) * 128, hl * 128:(hl + 1) * 128])
                    ld_bh[i] = max(v1, v2, v3)
                load_bh(0)
                st_free = [None, None]
                attn_free = [None, None]
                for i, (b, hl) in enumerate(bh_list):
                    bf = i % 2
                    if i + 1 < len(bh_list):
                        load_bh(i + 1)
                    wait("vector", "st", P.val("st"))
                    wait("vector", "mm", P.val("mm"))
                    wait("tensor", "ld", ld_bh[i])
                    for tqc in range(16):
                        wid_all = (tqc + 1) * 128
                        if tqc > 0 or i > 0:
                            wait("tensor", "sc", P.val("sc"))
                        nw = (wid_all + 511) // 512
                        tv = None
                        for w in range(nw):
                            wd = min(512, wid_all - 512 * w)
                            tv = op("tensor", lambda e, tqc=tqc, w=w, wd=wd, bf=bf:
                                    e.matmul(PS[:, 512 * w:512 * w + wd],
                                             QTB[bf][:, tqc * 128:(tqc + 1) * 128],
                                             KTB[bf][:, 512 * w:512 * w + wd],
                                             start=True, stop=True),
                                    sem=("mm" if w == nw - 1 else None))
                        wait("vector", "mm", tv)
                        offd = tqc * 128
                        op("vector", lambda e, offd=offd: e.tensor_tensor(
                            PS[:, offd:offd + 128], PS[:, offd:offd + 128], MASKT[:], OP.add))
                        vtrip(op("vector", lambda e, wid_all=wid_all, tqc=tqc: e.reduce_max(
                            MCOL[:, tqc:tqc + 1], PS[:, 0:wid_all], axis=AX.X), sem="ve"))
                        vng = op("vector", lambda e, tqc=tqc: e.tensor_scalar(
                            MNEG[:], MCOL[:, tqc:tqc + 1], -1.0, None, OP.mult), sem="ve")
                        wait("scalar", "ve", vng)
                        op("scalar", lambda e, wid_all=wid_all, tqc=tqc: e.activation(
                            ESCT[:, 0:wid_all], PS[:, 0:wid_all], AF.Exp,
                            bias=MNEG[:], accum_out=SCOL[:, tqc:tqc + 1]), sem="sc")
                    vsc = P.val("sc")
                    wait("vector", "sc", vsc)
                    for jd in range(2):
                        op("vector", lambda e, b=b, hl=hl, jd=jd: e.tensor_copy(
                            STILE[:, (2 * b + jd) * 32 + hl * 8: (2 * b + jd) * 32 + hl * 8 + 8],
                            SCOL[:, jd * 8:(jd + 1) * 8]))
                    vstok = op("vector", lambda e: e.tensor_copy(COLS[:, 15:16], COLS[:, 15:16]), sem="ve")
                    wait("tensor", "sc", vsc)
                    for tqc in range(16):
                        sl = tqc % 4
                        tv = op("tensor", lambda e, tqc=tqc, sl=sl: e.transpose(
                            PS[0:1, 1024 + sl * 128:1024 + sl * 128 + 128], MCOL[:, tqc:tqc + 1], IDENTF[:]), sem="mm")
                        wait("vector", "mm", tv)
                        vz = op("vector", lambda e, tqc=tqc, sl=sl: e.tensor_copy(
                            MRSB[:, tqc * 128:(tqc + 1) * 128], PS[0:1, 1024 + sl * 128:1024 + sl * 128 + 128]), sem="pz")
                        wait("tensor", "pz", vz)
                    for w in range(4):
                        tv = op("tensor", lambda e, w=w: e.matmul(
                            PS[:, 1024:1536], ONESF[:], MRSB[:, w * 512:(w + 1) * 512], start=True, stop=True), sem="mm")
                        wait("vector", "mm", tv)
                        vz = op("vector", lambda e, w=w: e.tensor_copy(MB[:, w * 512:(w + 1) * 512], PS[:, 1024:1536]), sem="pz")
                        wait("tensor", "pz", vz)
                    for w in range(4):
                        abk = 4 + (w % 2)
                        if attn_free[w % 2] is not None:
                            wait("tensor", "ve", attn_free[w % 2])
                        ntk = 4 * w + 4
                        for tkc in range(ntk):
                            sl = tkc % 2
                            sbk = 2 + sl
                            if st_free[sl] is not None:
                                wait("tensor", "ve", st_free[sl])
                            off = max(0, tkc * 128 - 512 * w)
                            tv = op("tensor", lambda e, tkc=tkc, w=w, off=off, sbk=sbk, bf=bf:
                                    e.matmul(PS[:, 512 * sbk + off:512 * sbk + 512],
                                             KTB[bf][:, tkc * 128:(tkc + 1) * 128],
                                             QTB[bf][:, 512 * w + off:512 * w + 512],
                                             start=True, stop=True), sem="mm")
                            wait("vector", "mm", tv)
                            if tkc * 128 >= 512 * w:
                                op("vector", lambda e, sbk=sbk, off=off: e.tensor_tensor(
                                    PS[:, 512 * sbk + off:512 * sbk + off + 128],
                                    PS[:, 512 * sbk + off:512 * sbk + off + 128], MASKTT[:], OP.add))
                            vs = op("vector", lambda e, sbk=sbk, off=off, w=w, sl=sl: e.scalar_tensor_tensor(
                                E1[sl][:, off:512], PS[:, 512 * sbk + off:512 * sbk + 512], LN127,
                                MB[:, 512 * w + off:512 * w + 512], OP.add, OP.subtract), sem="ve")
                            st_free[sl] = vs
                            wait("scalar", "ve", vs)
                            sv = op("scalar", lambda e, sl=sl, off=off: e.activation(
                                ESC3[sl][:, off:512], E1[sl][:, off:512], AF.Exp), sem="sc")
                            wait("vector", "sc", sv)
                            rv = op("vector", lambda e, sl=sl, off=off: e.tensor_scalar(
                                PINT[sl][:, off:512], ESC3[sl][:, off:512], MAGIC, -MAGIC, OP.add, OP.add), sem="ve")
                            wait("tensor", "ve", rv)
                            op("tensor", lambda e, tkc=tkc, sl=sl, off=off, abk=abk, bf=bf, ntk=ntk:
                               e.matmul(PS[:, 512 * abk + off:512 * abk + 512],
                                        VB[bf][:, tkc * 128:(tkc + 1) * 128],
                                        PINT[sl][:, off:512],
                                        start=(tkc == 0), stop=(tkc == ntk - 1)),
                               sem=("mm" if tkc == ntk - 1 else None))
                        tva = P.val("mm")
                        wait("vector", "mm", tva)
                        av = op("vector", lambda e, w=w, abk=abk: e.tensor_copy(
                            ATE[:, 512 * w:512 * (w + 1)], PS[:, 512 * abk:512 * abk + 512]), sem="ve")
                        attn_free[w % 2] = av
                    wait("sync", "ve", P.val("ve"))
                    for jd in range(2):
                        dma("sync", agai[2 * b + jd][hl * 128:(hl + 1) * 128, :],
                            ATE[:, jd * TO:(jd + 1) * TO], sem="st")
                    if "attn" in tap_t:
                        dma("sync", tap_t["attn"][hl * 128:(hl + 1) * 128, b * S:(b + 1) * S], ATE[:], sem="st")
                wait("sync", "ve", P.val("ve"))
                for d in range(NC):
                    dma("sync", agsi[d], STILE[:, d * 32:(d + 1) * 32], sem="st")
                if "ssum" in tap_t:
                    dma("sync", tap_t["ssum"][:], STILE[:], sem="st")
                wait("gpsimd", "st", P.val("st"))
                cc_a = ag(aga_in[:], aga_out[:])
                cc_s = ag(ags_in[:], ags_out[:])

        if build_upto >= 6:
            with contextlib.ExitStack() as ph:
                al = (lambda *a: ph.enter_context(sbt(*a))) if not flat else alloc
                P5A = al("p5_BIGA", [128, 4096], F32)
                P5B = al("p5_BIGB", [128, 4096], F32)
                P5FA = al("p5_BFA", [128, 4096], BF16)
                P5FB = al("p5_BFB", [128, 4096], BF16)
                INVS = al("p5_INVS", [128, 32 * NTC], F32)
                SWOB = al("p5_SWOB", [128, H], F32)
                SWR1 = al("p5_SWR1", [1, 512], F32)
                AITP = Buf(al("p5_AITP", [128, 2 * 4096], BF16), 4096)
                WOT = Buf(al("p5_WOT", [128, 4 * 8192], BF16), 8192)
                WOE = al("p5_WOE", [128, 512], F32)
                O1A = al("p5_O1A", [128, 4096], F32)
                O1B = al("p5_O1B", [128, 4096], F32)

                wait("sync", "cc", P.val("cc"))
                for r in range(NC):
                    def mka(e, r=r):
                        pid = get_pid(e)
                        srcb = agav[r * NC:(r + 1) * NC][bass.ts(pid, 1)]
                        return (agal[r:r + 1], srcb)
                    dyn_dma("sync", mka)
                    def mks(e, r=r):
                        pid = get_pid(e)
                        srcb = agsv[r * NC:(r + 1) * NC][bass.ts(pid, 1)]
                        return (agsl[r:r + 1], srcb)
                    dyn_dma("sync", mks)
                v_blk = P.val("ld")
                wait("sync", "ld", v_blk)
                for r in range(NC):
                    dma("sync", INVS[:, r * 32:(r + 1) * 32], agsl[r])
                v_invs = P.val("ld")
                wait("vector", "ld", v_invs)
                op("vector", lambda e: e.tensor_scalar(INVS[:], INVS[:], 127.0, None, OP.mult))
                vtrip(op("vector", lambda e: e.reciprocal(INVS[:], INVS[:]), sem="ve"))
                for r in range(NC):
                    if r > 0:
                        wait("sync", "mm", P.val("mm"))
                    v_swo = dma("sync", SWR1[:], agswo_out[r:r + 1, :])
                    wait("tensor", "ld", v_swo)
                    tv = op("tensor", lambda e, r=r: e.matmul(bank(0), ONESF[:], SWR1[:], start=True, stop=True), sem="mm")
                    wait("vector", "mm", tv)
                    vz = op("vector", lambda e, r=r: e.tensor_copy(SWOB[:, r * 512:(r + 1) * 512], bank(0)), sem="pz")
                    wait("tensor", "pz", vz)
                for tc in range(NTC):
                    wait("sync", "mm", P.val("mm"))
                    wait("sync", "ve", P.val("ve"))
                    vas = 0
                    for r in range(NC):
                        for gl in range(HPC):
                            g = r * HPC + gl
                            vas = dma("sync", P5FA[:, g * 128:(g + 1) * 128],
                                      agal[r][gl * 128:(gl + 1) * 128, tc * 128:(tc + 1) * 128])
                    wait("tensor", "ld", vas)
                    wait("tensor", "ve", P.val("ve"))
                    transpose_strip(lambda g: P5FA[:, g * 128:(g + 1) * 128],
                                    lambda g: P5FB[:, g * 128:(g + 1) * 128], 32)
                    wait("vector", "pz", P.val("pz"))
                    for g in range(32):
                        r2, hl2 = g // HPC, g % HPC
                        ic = r2 * 32 + hl2 * 8 + tc
                        op("vector", lambda e, g=g, ic=ic: e.tensor_scalar(
                            P5A[:, g * 128:(g + 1) * 128], P5FB[:, g * 128:(g + 1) * 128],
                            INVS[:, ic:ic + 1], None, OP.mult))
                    vtrip(op("vector", lambda e: e.reduce_max(COLS[:, 4:5], P5A[:], axis=AX.X, apply_absolute_value=True), sem="ve"))
                    vtrip(op("vector", lambda e, tc=tc: e.tensor_scalar(SACOL[:, tc:tc + 1], COLS[:, 4:5], 1.0 / 127.0, 1e-8, OP.mult, OP.max), sem="ve"))
                    vtrip(op("vector", lambda e, tc=tc: e.reciprocal(COLS[:, 6:7], SACOL[:, tc:tc + 1]), sem="ve"))
                    op("vector", lambda e: e.tensor_scalar(P5B[:], P5A[:], COLS[:, 6:7], MAGIC, OP.mult, OP.add))
                    vi = op("vector", lambda e: e.tensor_scalar(P5FA[:], P5B[:], -MAGIC, None, OP.add), sem="ve")
                    wait("tensor", "ve", vi)
                    transpose_strip(lambda g: P5FA[:, 128 * g:128 * (g + 1)],
                                    lambda g: P5FB[:, g * 128:(g + 1) * 128], 32)
                    wait("sync", "pz", P.val("pz"))
                    dma("sync", ait_d[tc], P5FB[:], sem="st")
                    if "aint" in tap_t:
                        dma("sync", tap_t["aint"][tc], P5FB[:], sem="st")
                wait("sync", "cc", cc_wo)
                wo_mark = {}
                for tcp in range(4):
                    tc0, tc1 = 2 * tcp, 2 * tcp + 1
                    wait("sync", "mm", P.val("mm"))
                    va0 = dma("sync", AITP[0], ait_d[tc0])
                    va1 = dma("sync", AITP[1], ait_d[tc1])
                    wait("tensor", "ld", max(va0, va1))
                    for rg in range(4):
                        r0, r1 = 2 * rg, 2 * rg + 1
                        prev_mark = wo_mark.get("last")
                        if prev_mark:
                            wait("scalar", "mm", prev_mark)
                        woldv = {}
                        for q4 in range(4):
                            lv = 0
                            for ri, r in enumerate((r0, r1)):
                                dstw = WOT[q4][:, ri * 4096:(ri + 1) * 4096].rearrange("p (g f) -> p g f", g=8)
                                srcw = agwov[r][8 * q4:8 * (q4 + 1)].rearrange("g p f -> p g f")
                                lv = dma("scalar", dstw, srcw)
                            woldv[q4] = lv
                        wait("tensor", "ve", P.val("ve"))
                        two = None
                        for q4 in range(4):
                            wait("tensor", "ld2", woldv[q4])
                            for g8 in range(8):
                                g = q4 * 8 + g8
                                for ti in range(2):
                                    for ri in range(2):
                                        last = (g == 31 and ti == 1 and ri == 1)
                                        two2 = op("tensor", lambda e, g=g, g8=g8, ti=ti, ri=ri, q4=q4:
                                                  e.matmul(bank(2 * ti + ri),
                                                           AITP[ti][:, g * 128:(g + 1) * 128],
                                                           WOT[q4][:, ri * 4096 + g8 * 512: ri * 4096 + (g8 + 1) * 512],
                                                           start=(g == 0), stop=(g == 31)),
                                                  sem=("mm" if last else None))
                                        if last:
                                            two = two2
                        wo_mark["last"] = two
                        wait("vector", "mm", two)
                        for ti, tcx in enumerate((tc0, tc1)):
                            dstt = O1A if ti == 0 else O1B
                            for ri, r in enumerate((r0, r1)):
                                op("vector", lambda e, ti=ti, ri=ri, tcx=tcx: e.tensor_scalar(
                                    WOE[:], bank(2 * ti + ri), SACOL[:, tcx:tcx + 1], None, OP.mult))
                                op("vector", lambda e, dstt=dstt, r=r: e.tensor_tensor(
                                    dstt[:, r * 512:(r + 1) * 512], WOE[:], SWOB[:, r * 512:(r + 1) * 512], OP.mult))
                        op("vector", lambda e: e.tensor_copy(COLS[:, 15:16], COLS[:, 15:16]), sem="ve")
                    for ti, tcx in enumerate((tc0, tc1)):
                        srct = O1A if ti == 0 else O1B
                        wait("sync", "ve", P.val("ve"))
                        vl = dma("sync", P5A[:], xv[tcx])
                        wait("vector", "ld", vl)
                        if "o1" in tap_t:
                            vv1 = op("vector", lambda e, srct=srct: e.tensor_copy(P5B[:], srct[:]), sem="ve")
                            wait("sync", "ve", vv1)
                            dma("sync", tap_t["o1"][tcx], P5B[:], sem="st")
                        op("vector", lambda e, srct=srct: e.tensor_tensor(P5B[:], srct[:], P5A[:], OP.add))
                        vz = op("vector", lambda e: e.reduce_sum(COLS[:, 0:1], P5B[:], axis=AX.X), sem="ve")
                        vtrip(vz)
                        wait("scalar", "ve", vz)
                        sv = op("scalar", lambda e: e.activation(P5A[:], P5B[:], AF.Square, accum_out=COLS[:, 1:2]), sem="sc")
                        wait("vector", "sc", sv)
                        op("vector", lambda e: e.tensor_scalar(COLS[:, 2:3], COLS[:, 0:1], 1.0 / H, None, OP.mult))
                        vv9 = op("vector", lambda e: e.tensor_scalar(COLS[:, 3:4], COLS[:, 1:2], 1.0 / H, None, OP.mult), sem="ve")
                        vtrip(vv9)
                        vv9 = op("vector", lambda e: e.tensor_scalar(COLS[:, 4:5], COLS[:, 2:3], COLS[:, 2:3], None, OP.mult), sem="ve")
                        vtrip(vv9)
                        vz2 = op("vector", lambda e: e.tensor_scalar(COLS[:, 6:7], COLS[:, 3:4], COLS[:, 4:5], EPS, OP.subtract, OP.add), sem="ve")
                        vtrip(vz2)
                        wait("scalar", "ve", vz2)
                        sv2 = op("scalar", lambda e: e.activation(COLS[:, 7:8], COLS[:, 6:7], AF.Sqrt), sem="sc")
                        wait("vector", "sc", sv2)
                        vtrip(op("vector", lambda e: e.reciprocal(COLS[:, 8:9], COLS[:, 7:8]), sem="ve"))
                        vy = op("vector", lambda e: e.tensor_scalar(P5A[:], P5B[:], COLS[:, 2:3], COLS[:, 8:9], OP.subtract, OP.mult), sem="ve")
                        wait("sync", "ve", vy)
                        dma("sync", yf_d[tcx], P5A[:], sem="st")
                        if "y" in tap_t:
                            dma("sync", tap_t["y"][tcx], P5A[:], sem="st")
                        vyb = op("vector", lambda e: e.tensor_copy(P5FA[:], P5A[:]), sem="ve")
                        wait("tensor", "ve", vyb)
                        transpose_strip(lambda g: P5FA[:, 128 * g:128 * (g + 1)],
                                        lambda g: P5FB[:, g * 128:(g + 1) * 128], 32)
                        wait("sync", "pz", P.val("pz"))
                        dma("sync", yt_d[tcx], P5FB[:], sem="st")

        if build_upto >= 7:
            with contextlib.ExitStack() as ph:
                al = (lambda *a: ph.enter_context(sbt(*a))) if not flat else alloc
                YT = al("p6_YT", [128, 32 * 1024], BF16)
                ZSGU = al("p6_ZSGU", [32, 2 * NC * 1408], BF16)
                YGSB = al("p6_YGSB", [32, 1024], BF16)
                GWB = Buf(al("p6_GWB", [128, 2 * 4096], BF16), 4096)
                UWB = Buf(al("p6_UWB", [128, 2 * 4096], BF16), 4096)
                HCH = al("p6_HCH", [128, 1024], BF16)
                HSB = al("p6_HSB", [128, 1024], BF16)

                wait("sync", "st", P.val("st"))
                for tc in range(NTC):
                    for g in range(32):
                        dma("sync", YT[:, g * 1024 + tc * 128: g * 1024 + (tc + 1) * 128],
                            yt_d[tc][:, g * 128:(g + 1) * 128])
                for r in range(NC):
                    dma("sync", ZSGU[:, r * 1408:(r + 1) * 1408],
                        agz_out[r:r + 1, 0:ZSZ].rearrange("o (p f) -> (o p) f", p=32))
                    dma("sync", ZSGU[:, (NC + r) * 1408:(NC + r + 1) * 1408],
                        agz_out[r:r + 1, ZSZ:2 * ZSZ].rearrange("o (p f) -> (o p) f", p=32))
                v_zs = P.val("ld")
                wait("sync", "cc", cc_f)
                wait("tensor", "ld", v_zs)
                tyg = None
                for g in range(32):
                    for th in range(2):
                        tyg = op("tensor", lambda e, g=g, th=th: e.matmul(
                            PS[0:32, 512 * (4 + th):512 * (4 + th) + 512],
                            ZC32[:, 32 - g:64 - g], YT[:, g * 1024 + th * 512: g * 1024 + (th + 1) * 512],
                            start=(g == 0), stop=(g == 31)),
                            sem=("mm" if (g == 31 and th == 1) else None))
                wait("vector", "mm", tyg)
                op("vector", lambda e: e.tensor_copy(YGSB[:, 0:512], PS[0:32, 2048:2560]))
                vyg = op("vector", lambda e: e.tensor_copy(YGSB[:, 512:1024], PS[0:32, 2560:3072]), sem="ve")
                wait("tensor", "ve", vyg)
                rc = real_counts()
                glist = [(r, lc) for r in range(NC) for lc in range(rc[r])]
                gld = {}
                gu_mark = {}

                def load_gu(j):
                    r, lc = glist[j]
                    bufj = j % 2
                    if j >= 2:
                        wait("scalar", "mm", gu_mark[j - 2])
                        wait("gpsimd", "mm", gu_mark[j - 2])
                    v1 = dma("scalar", GWB[bufj], agg_out[r:r + 1, lc * 128 * H: (lc + 1) * 128 * H].rearrange("o (p f) -> (o p) f", p=128))
                    v2 = dma("gpsimd", UWB[bufj], agu_out[r:r + 1, lc * 128 * H: (lc + 1) * 128 * H].rearrange("o (p f) -> (o p) f", p=128))
                    gld[j] = (v1, v2)
                wait("gpsimd", "st", P.val("st"))
                wait("gpsimd", "mm", P.val("mm"))
                load_gu(0)
                for j, (r, lc) in enumerate(glist):
                    bufj = j % 2
                    gi_real = sum(rc[:r]) + lc
                    wait("vector", "st", P.val("st"))
                    wait("tensor", "ld2", gld[j][0])
                    wait("tensor", "ld3", gld[j][1])
                    if j >= 1:
                        wait("tensor", "ve", P.val("ve"))
                    tgu = None
                    for mi in range(2):
                        wb = GWB if mi == 0 else UWB
                        for th in range(2):
                            bnk = 2 * mi + th
                            for g in range(32):
                                op("tensor", lambda e, g=g, th=th, bnk=bnk, wb=wb, bufj=bufj: e.matmul(
                                    bank(bnk), wb[bufj][:, g * 128:(g + 1) * 128],
                                    YT[:, g * 1024 + th * 512: g * 1024 + (th + 1) * 512],
                                    start=(g == 0), stop=False))
                            zoff = (0 if mi == 0 else NC) * 1408 + r * 1408 + lc * 128
                            tgu = op("tensor", lambda e, zoff=zoff, th=th, bnk=bnk: e.matmul(
                                bank(bnk), ZSGU[0:32, zoff:zoff + 128],
                                YGSB[0:32, th * 512:(th + 1) * 512], start=False, stop=True),
                                sem=("mm" if (mi == 1 and th == 1) else None))
                    gu_mark[j] = tgu
                    if j + 1 < len(glist):
                        load_gu(j + 1)
                    wait("scalar", "mm", tgu)
                    for th in range(2):
                        sv = op("scalar", lambda e, th=th: e.activation(HSB[:, th * 512:(th + 1) * 512], bank(th), AF.Silu), sem="sc")
                        wait("vector", "sc", sv)
                        op("vector", lambda e, th=th: e.tensor_tensor(HCH[:, th * 512:(th + 1) * 512], HSB[:, th * 512:(th + 1) * 512], bank(2 + th), OP.mult))
                    vh = op("vector", lambda e: e.tensor_copy(COLS[:, 15:16], COLS[:, 15:16]), sem="ve")
                    wait("sync", "ve", vh)
                    dma("sync", ht_d[gi_real], HCH[:], sem="st")
                    if "ht" in tap_t:
                        dma("sync", tap_t["ht"][gi_real], HCH[:], sem="st")

            with contextlib.ExitStack() as ph:
                al = (lambda *a: ph.enter_context(sbt(*a))) if not flat else alloc
                FT = al("p6d_FT", [128, NTC * 4096], BF16)
                DWB = Buf(al("p6d_DWB", [128, 2 * NCHP * 512], BF16), NCHP * 512)
                HTB = Buf(al("p6d_HTB", [128, 2 * 512], BF16), 512)
                DOB = al("p6d_DOB", [128, 512], BF16)
                L2A = al("p6d_BIGA", [128, 4096], F32)
                L2B = al("p6d_BIGB", [128, 4096], F32)
                L2C = al("p6d_BIGC", [128, 4096], F32)

                wait("sync", "st", P.val("st"))
                rc = real_counts()
                nreal = sum(rc)
                dn_mark = {}
                for ow in range(8):
                    for th in range(2):
                        dwl = {}
                        wait("scalar", "mm", P.val("mm"))
                        wait("gpsimd", "mm", P.val("mm"))
                        wait("sync", "mm", P.val("mm"))

                        def load_dw(r2, idx):
                            off = ow * (128 * NCHP * 512)
                            q = "scalar" if idx % 2 == 0 else "gpsimd"
                            if idx >= 2:
                                wait(q, "mm", dn_mark[(ow, th, idx - 2)])
                            dwl[r2] = (q, dma(q, DWB[idx % 2],
                                       agd_out[r2:r2 + 1, off:off + 128 * NCHP * 512].rearrange("o (p f) -> (o p) f", p=128)))
                        load_dw(0, 0)
                        wait("tensor", "ve", P.val("ve"))
                        cnt = 0
                        tdn = None
                        for r in range(NC):
                            if r + 1 < NC:
                                load_dw(r + 1, r + 1)
                            wait("tensor", {"scalar": "ld2", "gpsimd": "ld3"}[dwl[r][0]], dwl[r][1])
                            for lc in range(rc[r]):
                                gi_real = sum(rc[:r]) + lc
                                hbuf = cnt % 2
                                if cnt >= 2:
                                    wait("sync", "mm", dn_mark.get((ow, th, "h", cnt - 2), 0))
                                hv = dma("sync", HTB[hbuf], ht_d[gi_real][:, th * 512:(th + 1) * 512])
                                wait("tensor", "ld", hv)
                                for oc4 in range(4):
                                    tdn = op("tensor", lambda e, lc=lc, oc4=oc4, r=r, hbuf=hbuf, cnt=cnt:
                                             e.matmul(bank(oc4), DWB[r % 2][:, lc * 512 + oc4 * 128: lc * 512 + (oc4 + 1) * 128],
                                                      HTB[hbuf], start=(cnt == 0), stop=(cnt == nreal - 1)),
                                             sem=("mm" if oc4 == 3 else None))
                                dn_mark[(ow, th, "h", cnt)] = tdn
                                cnt += 1
                            dn_mark[(ow, th, r)] = tdn
                        wait("vector", "mm", tdn)
                        for oc4 in range(4):
                            oc = ow * 4 + oc4
                            vv = op("vector", lambda e, oc4=oc4: e.tensor_copy(DOB[:], bank(oc4)), sem="ve")
                            wait("tensor", "ve", vv)
                            transpose_strip(lambda g: DOB[:, 128 * g:128 * (g + 1)],
                                            lambda g, oc=oc, th=th: FT[:, (th * 4 + g) * 4096 + oc * 128: (th * 4 + g) * 4096 + oc * 128 + 128], 4)
                v_ft = P.val("pz")
                wait("vector", "pz", v_ft)
                outv = out_own.rearrange("(c p) d -> c p d", p=128)
                for tc in range(NTC):
                    wait("sync", "ve", P.val("ve"))
                    vl = dma("sync", L2A[:], yf_d[tc])
                    wait("vector", "ld", vl)
                    op("vector", lambda e, tc=tc: e.tensor_tensor(L2B[:], FT[:, tc * 4096:(tc + 1) * 4096], L2A[:], OP.add))
                    vz = op("vector", lambda e: e.reduce_sum(COLS[:, 0:1], L2B[:], axis=AX.X), sem="ve")
                    vtrip(vz)
                    wait("scalar", "ve", vz)
                    sv = op("scalar", lambda e: e.activation(L2C[:], L2B[:], AF.Square, accum_out=COLS[:, 1:2]), sem="sc")
                    wait("vector", "sc", sv)
                    op("vector", lambda e: e.tensor_scalar(COLS[:, 2:3], COLS[:, 0:1], 1.0 / H, None, OP.mult))
                    vv9 = op("vector", lambda e: e.tensor_scalar(COLS[:, 3:4], COLS[:, 1:2], 1.0 / H, None, OP.mult), sem="ve")
                    vtrip(vv9)
                    vv9 = op("vector", lambda e: e.tensor_scalar(COLS[:, 4:5], COLS[:, 2:3], COLS[:, 2:3], None, OP.mult), sem="ve")
                    vtrip(vv9)
                    vz2 = op("vector", lambda e: e.tensor_scalar(COLS[:, 6:7], COLS[:, 3:4], COLS[:, 4:5], EPS, OP.subtract, OP.add), sem="ve")
                    vtrip(vz2)
                    wait("scalar", "ve", vz2)
                    sv2 = op("scalar", lambda e: e.activation(COLS[:, 7:8], COLS[:, 6:7], AF.Sqrt), sem="sc")
                    wait("vector", "sc", sv2)
                    vtrip(op("vector", lambda e: e.reciprocal(COLS[:, 8:9], COLS[:, 7:8]), sem="ve"))
                    vo = op("vector", lambda e: e.tensor_scalar(L2C[:], L2B[:], COLS[:, 2:3], COLS[:, 8:9], OP.subtract, OP.mult), sem="ve")
                    wait("sync", "ve", vo)
                    dma("sync", outv[tc], L2C[:], sem="st")

        wait("sync", "st", P.val("st"))
        wait("sync", "ld", P.val("ld"))
        wait("sync", "ld2", P.val("ld2"))
        wait("sync", "ld3", P.val("ld3"))
        wait("sync", "cc", P.val("cc"))

        emit_all(block, P, SEM)
    return nc, tap_t


def emit_all(block, P, SEM):
    @block.sync
    def _(e):
        for k, fn in P.items:
            if k == "sync":
                fn(e, SEM)

    @block.scalar
    def _(e):
        for k, fn in P.items:
            if k == "scalar":
                fn(e, SEM)

    @block.vector
    def _(e):
        for k, fn in P.items:
            if k == "vector":
                fn(e, SEM)

    @block.tensor
    def _(e):
        for k, fn in P.items:
            if k == "tensor":
                fn(e, SEM)

    @block.gpsimd
    def _(e):
        for k, fn in P.items:
            if k == "gpsimd":
                fn(e, SEM)


def _mk_e64():
    e = np.zeros((64, 64 * 128), ml_dtypes.bfloat16)
    for g in range(64):
        e[g, g * 128:(g + 1) * 128] = 1.0
    return e


def _host_consts():
    t = np.arange(S, dtype=np.float32)
    inv = 1.0 / (10000.0 ** (np.arange(0, D, 2, dtype=np.float32) / D))
    fr = np.einsum("i,j->ij", t, inv)
    emb = np.concatenate([fr, fr], -1)
    mt = np.where(np.arange(128)[None, :] <= np.arange(128)[:, None], 0.0, NEG).astype(np.float32)
    zc = np.zeros((128, 96), ml_dtypes.bfloat16)
    zc[:, 32] = 1.0
    return dict(cos_t=np.cos(emb).astype(np.float32), sin_t=np.sin(emb).astype(np.float32),
                ident=np.eye(128, dtype=ml_dtypes.bfloat16), identf=np.eye(128, dtype=np.float32),
                ones_f=np.ones((1, 128), np.float32), ones_b=np.ones((1, 128), ml_dtypes.bfloat16),
                ones_a=np.ones((128, 128), ml_dtypes.bfloat16),
                e64=_mk_e64(),
                mask_tok=mt, mask_T=np.ascontiguousarray(mt.T), zc32=zc)


def host_inputs(inputs):
    xs = np.ascontiguousarray(np.asarray(inputs["L0_attn_hidden_states"]).reshape(T, H))
    consts = _host_consts()
    maps = []
    for c in range(NC):
        m = {"x_own": np.ascontiguousarray(xs[TO * c:TO * (c + 1)])}
        for k in "qkvo":
            m[f"w_{k}"] = np.ascontiguousarray(np.asarray(inputs[f"L0_attn_w{k}"])[512 * c:512 * (c + 1)])
        cks = ffn_chunks(c)
        npad = NCHP - len(cks)

        def colsel(a, w):
            idx = [w * g + j for g in cks for j in range(w)]
            out = np.asarray(a)[:, idx]
            if npad:
                out = np.pad(out, [(0, 0), (0, w * npad)])
            return np.ascontiguousarray(out)

        m["g_qw"] = colsel(inputs["L0_ffn_gate_qweight"], 16).astype("<i4").view(np.uint8).reshape(H, NCHP * 64).copy()
        m["g_sc"] = colsel(inputs["L0_ffn_gate_scales"], 128)
        m["g_qz"] = colsel(inputs["L0_ffn_gate_qzeros"], 16).astype("<i4").view(np.uint8).reshape(32, NCHP * 64).copy()
        m["u_qw"] = colsel(inputs["L0_ffn_up_qweight"], 16).astype("<i4").view(np.uint8).reshape(H, NCHP * 64).copy()
        m["u_sc"] = colsel(inputs["L0_ffn_up_scales"], 128)
        m["u_qz"] = colsel(inputs["L0_ffn_up_qzeros"], 16).astype("<i4").view(np.uint8).reshape(32, NCHP * 64).copy()
        ridx = [128 * g + j for g in cks for j in range(128)]
        dq = np.asarray(inputs["L0_ffn_down_qweight"])[ridx, :]
        dq = np.pad(dq, [(0, 128 * npad), (0, 0)])
        m["d_qw"] = np.ascontiguousarray(dq).astype("<i4").view(np.uint8).reshape(NCHP * 128, 2048).copy()
        m["d_sc"] = np.ascontiguousarray(np.pad(np.asarray(inputs["L0_ffn_down_scales"])[cks, :], [(0, npad), (0, 0)]))
        m["d_qz"] = np.ascontiguousarray(np.pad(np.asarray(inputs["L0_ffn_down_qzeros"])[cks, :], [(0, npad), (0, 0)])).astype("<i4").view(np.uint8).reshape(NCHP, 2048).copy()
        m.update(consts)
        maps.append(m)
    return maps


def kernel(**inputs):
    maps = host_inputs(inputs)
    prog, _ = build_program()
    res = run_bass_kernel_spmd(prog, maps, core_ids=list(range(NC)))
    out = np.concatenate([res.results[c]["out_own"] for c in range(NC)], axis=0)
    return out.reshape(B, S, H).astype(np.float32)
